# version 14
# baseline (speedup 1.0000x reference)
import math
from contextlib import ExitStack

import numpy as np
import concourse.bass as bass
import concourse.mybir as mybir
from concourse.bass_utils import run_bass_kernel_spmd

F32 = mybir.dt.float32
BF16 = mybir.dt.bfloat16
I32 = mybir.dt.int32
U32 = mybir.dt.uint32
AF = mybir.ActivationFunctionType
ALU = mybir.AluOpType
AX = mybir.AxisListType

LN_EPS = 1e-5
DEQ = 1.0 / 127.0
M_MARGIN = 3.0
SKIP_MARGIN = 92.0
WQ_CAP = 64.0
KILL = 30000.0


def alibi_slopes(h):
    closest = 2 ** math.floor(math.log2(h))
    base = 2 ** (-(2 ** -(math.log2(closest) - 3)))
    slopes = base ** np.arange(1, closest + 1)
    if closest != h:
        eb = 2 ** (-(2 ** -(math.log2(2 * closest) - 3)))
        extra = eb ** np.arange(1, 2 * (h - closest) + 1, 2)
        slopes = np.concatenate([slopes, extra])
    return slopes.astype(np.float64)


class Geom:
    def __init__(self, T=8192, depth=4):
        self.B, self.T, self.D, self.H, self.HD = 2, T, 1024, 16, 64
        self.L, self.F, self.FG = depth, 4096, 3072
        self.NC, self.GRP = 8, 4
        self.CH = 512
        assert T % (4 * self.CH) == 0
        self.NCH = T // self.CH
        self.NS = self.NCH // 4
        self.TLOC = self.NS * self.CH
        self.NKT = T // 128
        self.NQS = self.TLOC // 128
        self.DCH = self.D // 128
        self.TC = 512
        self.NT = self.TLOC // self.TC
        self.slopes = alibi_slopes(self.H)
        self.Wq = []
        for sl in self.slopes:
            w = 128
            while w < min(512, self.CH) and sl * (2 * w - 1) <= WQ_CAP:
                w *= 2
            self.Wq.append(w)
        self.nslot = [
            min(self.NKT, int((SKIP_MARGIN / sl + w - 1) // 128) + 1)
            for sl, w in zip(self.slopes, self.Wq)
        ]
        self.nmat = {}
        for h in range(self.H):
            w = self.Wq[h]
            for s in range(self.NS):
                for u in range(self.CH // w):
                    caus = ((4 * s + 3) * self.CH + (u + 1) * w) // 128
                    self.nmat[(h, s, u)] = min(self.nslot[h], caus)
        self.bmin, self.nbuf = {}, {}
        for h in range(self.H):
            beta = self.Wq[h] // 128
            bm, bx = 10 ** 9, -10 ** 9
            for s in range(self.NS):
                w = self.Wq[h]
                for u in range(self.CH // w):
                    n = self.nmat[(h, s, u)]
                    bm = min(bm, 16 * s + (u + 1) * (w // 128) - n)
                    bx = max(bx, 16 * s + (u + 1) * (w // 128) - 1)
            self.bmin[h] = bm
            self.nbuf[h] = bx - bm + 1
        border = sorted(range(self.H // 2),
                        key=lambda b: (-max(self.nbuf[2 * b],
                                            self.nbuf[2 * b + 1]), b))
        border = border[-1:] + border[:-1]
        self.pairs = [(2 * b, 2 * b + 1) for b in border]
        self.bias_col = {}
        nb = 0
        for h in range(self.H):
            w = self.Wq[h]
            for s in range(self.NS):
                for u in range(self.CH // w):
                    for i in range(self.nmat[(h, s, u)]):
                        self.bias_col[(h, s, u, i)] = nb
                        nb += 1
        self.NBC = nb
        for (ha, hb) in self.pairs:
            assert ha // 2 == hb // 2, "pair must map to one AG block"
        self.VROW = 65
        self.VSZ = self.TLOC * self.VROW
        self.KSZ = 64 * self.TLOC
        lcm = 33280
        raw = 2 * (self.VSZ + self.KSZ)
        self.PBLK = ((raw + lcm - 1) // lcm) * lcm
        self.SHARD = 8 * self.PBLK
        assert self.PBLK % 512 == 0 and self.PBLK % 260 == 0
        assert self.VSZ % 260 == 0 and (2 * self.VSZ) % 512 == 0
        assert self.KSZ % 512 == 0
        self.padlo = {}
        for h in range(self.H):
            al = (-self.bmin[h]) % 4
            self.padlo[h] = (4 - al) % 4 if al != 0 else 0
        self.idx_plan = []
        self.kload2 = {}
        for b in range(self.H // 2):
            ngrp = max((self.nbuf[2 * b + a] + self.padlo[2 * b + a] + 3)
                       // 4 for a in range(2))
            self.kload2[b] = []
            for m in range(ngrp):
                self.idx_plan.append(("K2", b, m))
                self.kload2[b].append((4 * m, len(self.idx_plan) - 1))
        self.vload = {}
        for h in range(self.H):
            ngrp = (self.nbuf[h] + self.padlo[h] + 3) // 4
            self.vload[h] = []
            for m in range(ngrp):
                self.idx_plan.append(("V", h, m))
                self.vload[h].append((4 * m, len(self.idx_plan) - 1))
        self.NIDX = len(self.idx_plan)
        self.NBK = max(4 * ((self.nbuf[h] + self.padlo[h] + 3) // 4)
                       for h in range(self.H))

    def bslot(self, h, s, u, i):
        beta = self.Wq[h] // 128
        return 16 * s + (u + 1) * beta - 1 - i - self.bmin[h] + self.padlo[h]

    def mask_cols(self):
        return [(512, i) for i in range(4)]


def host_consts(g):
    mc = g.mask_cols()
    CC = sum(w for w, i in mc) + 128 + 8 * 16 + 1
    cp = np.zeros((128, CC), dtype=np.float32)
    p = np.arange(128)
    off = 0
    for (w, i) in mc:
        q = np.arange(w)[None, :]
        cp[:, off:off + w] = (q >= (p[:, None] + w - 128 * (i + 1)))
        off += w
    cp[:, off:off + 128] = np.eye(128, dtype=np.float32)
    off += 128
    for c in range(8):
        blk = np.zeros((128, 16), dtype=np.float32)
        for pp in range(128):
            blk[pp, 2 * c + (1 if pp >= 64 else 0)] = 1.0
        cp[:, off + 16 * c: off + 16 * (c + 1)] = blk
    off += 128
    cp[:, off] = p
    ramp = np.zeros((128, g.NBC), dtype=np.float32)
    for (h, s, u, i), col in g.bias_col.items():
        w = g.Wq[h]
        ramp[:, col] = g.slopes[h] * (p + w - 128 * (i + 1))
    return cp, ramp


class _Op:
    __slots__ = ("idx", "engine", "emit", "reads", "writes", "deps",
                 "need_inc", "inc_value", "inc_amt", "is_dma", "queue", "tag")

    def __init__(self, idx, engine, emit, reads, writes, is_dma, queue, tag,
                 inc_amt):
        self.idx, self.engine, self.emit = idx, engine, emit
        self.reads, self.writes = reads, writes
        self.deps = set()
        self.need_inc, self.inc_value = False, None
        self.inc_amt = inc_amt
        self.is_dma, self.queue, self.tag = is_dma, queue, tag


_COMPUTE = ("tensor", "vector", "scalar", "gpsimd")
_QUEUES = ("syncq", "gpsq", "scalq", "ccq")
_QOF = {"sync": "syncq", "gpsimd": "gpsq", "scalar": "scalq"}


class Sched:
    def __init__(self, nc):
        self.nc = nc
        self.ops = []
        self.last_writer = {}
        self.readers = {}
        self.pending = {}

    def add(self, engine, emit, reads=(), writes=(), dma=False, cc=False,
            tag=""):
        queue = "ccq" if cc else (_QOF[engine] if dma else None)
        op = _Op(len(self.ops), engine, emit, list(reads), list(writes),
                 dma or cc, queue, tag, 1 if cc else (16 if dma else 1))
        for r in op.reads:
            w = self.last_writer.get(r)
            if w is not None:
                op.deps.add(w)
        for w in op.writes:
            pw = self.last_writer.get(w)
            if pw is not None:
                op.deps.add(pw)
            for rd in self.readers.get(w, ()):
                op.deps.add(rd)
            if w in self.pending:
                op.deps |= self.pending.pop(w)
        op.deps.discard(op.idx)
        self.ops.append(op)
        for r in op.reads:
            self.readers.setdefault(r, []).append(op.idx)
        for w in op.writes:
            self.last_writer[w] = op.idx
            self.readers[w] = []
        return op.idx

    def barrier(self, old_regions, new_regions):
        dep = set()
        for r in old_regions:
            if r in self.last_writer:
                dep.add(self.last_writer[r])
            dep.update(self.readers.get(r, ()))
        for r in new_regions:
            self.pending.setdefault(r, set()).update(dep)

    DMA_LANES = 6

    def finalize(self, final_engine="sync"):
        nc, ops = self.nc, self.ops
        for op in ops:
            for d in op.deps:
                ops[d].need_inc = True
        for op in ops:
            if op.is_dma:
                op.need_inc = True
        lane_count = {}
        for op in ops:
            if op.is_dma and op.queue != "ccq":
                j = lane_count.get(op.queue, 0)
                lane_count[op.queue] = j + 1
                op.queue = f"{op.queue}{j % self.DMA_LANES}"
        counters = {}
        for op in ops:
            dom = op.queue if op.is_dma else op.engine
            if op.need_inc:
                counters[dom] = counters.get(dom, 0) + op.inc_amt
                op.inc_value = counters[dom]
        self.final_counts = dict(counters)

        with ExitStack() as st:
            sem_names = list(_COMPUTE) + ["ccq"] + [
                f"{q}{r}" for q in ("syncq", "gpsq", "scalq")
                for r in range(self.DMA_LANES)]
            sems = {n: st.enter_context(nc.semaphore("sem_" + n))
                    for n in sem_names}
            block = st.enter_context(nc.Block())
            by_eng = {e: [] for e in
                      ("tensor", "vector", "scalar", "gpsimd", "sync")}
            for op in ops:
                by_eng[op.engine].append(op)
            wm = {}

            def body_for(ename, elist):
                def body(eng):
                    for op in elist:
                        tgt = {}
                        for d in op.deps:
                            p = ops[d]
                            dom = p.queue if p.is_dma else p.engine
                            if (not p.is_dma) and p.engine == op.engine \
                                    and p.engine == "tensor":
                                continue
                            v = p.inc_value
                            if wm.get((ename, dom), 0) >= v:
                                continue
                            tgt[dom] = max(tgt.get(dom, 0), v)
                        if op.is_dma and op.queue != "ccq" \
                                and op.inc_value > 16:
                            dom = op.queue
                            v = op.inc_value - 16
                            if wm.get((ename, dom), 0) < v:
                                tgt[dom] = max(tgt.get(dom, 0), v)
                        for dom in sorted(tgt):
                            eng.wait_ge(sems[dom], tgt[dom])
                            wm[(ename, dom)] = tgt[dom]
                        inst = op.emit(eng)
                        if op.need_inc:
                            dom = op.queue if op.is_dma else op.engine
                            assert inst is not None, f"op needs inst: {op.tag}"
                            inst.then_inc(sems[dom], op.inc_amt)
                    if ename == final_engine:
                        for dom in sorted(counters):
                            eng.wait_ge(sems[dom], counters[dom])
                return body

            for ename in ("sync", "tensor", "vector", "scalar", "gpsimd"):
                getattr(block, ename)(body_for(ename, by_eng[ename]))


def build_program(g: Geom):
    nc = bass.Bass("TRN2", target_bir_lowering=False, debug=False)
    D, H, L, F, FG = g.D, g.H, g.L, g.F, g.FG
    TLOC, TC, NT, NS, DCH, NQS = g.TLOC, g.TC, g.NT, g.NS, g.DCH, g.NQS
    NBK = g.NBK
    HSF = min(1024, TLOC)
    NTH = TLOC // HSF
    NSUB = HSF // TC

    x_h = nc.dram_tensor("x_slab", [TLOC, D], F32, kind="ExternalInput").ap()
    wqkv_h = nc.dram_tensor("wqkv", [L, D, 3 * D], F32, kind="ExternalInput").ap()
    wo_h = nc.dram_tensor("wo", [L, D, D], F32, kind="ExternalInput").ap()
    w1_h = nc.dram_tensor("w1", [L, D, F], F32, kind="ExternalInput").ap()
    w2_h = nc.dram_tensor("w2", [L, F, D], F32, kind="ExternalInput").ap()
    wg_h = nc.dram_tensor("wg", [D, FG], I32, kind="ExternalInput").ap()
    wu_h = nc.dram_tensor("wu", [D, FG], I32, kind="ExternalInput").ap()
    wd_h = nc.dram_tensor("wd", [FG, D], I32, kind="ExternalInput").ap()
    PCL = 32 + 16 + F // 128
    PCOLS = L * PCL + 16
    params_h = nc.dram_tensor("params", [128, PCOLS], F32,
                              kind="ExternalInput").ap()
    brow_h = nc.dram_tensor("brow", [3, L * D], F32, kind="ExternalInput").ap()
    flag_h = nc.dram_tensor("flagrow", [1, g.NBC], F32,
                            kind="ExternalInput").ap()
    idx_h = nc.dram_tensor("idxtab", [128, g.NIDX], U32,
                           kind="ExternalInput").ap()
    mc = g.mask_cols()
    CC = sum(w for w, i in mc) + 128 + 8 * 16 + 1
    cpack_h = nc.dram_tensor("cpack", [128, CC], F32,
                             kind="ExternalInput").ap()
    rampall_h = nc.dram_tensor("rampall", [128, g.NBC], F32,
                               kind="ExternalInput").ap()
    out_h = nc.dram_tensor("out_slab", [TLOC, D], F32,
                           kind="ExternalOutput").ap()

    agin = nc.dram_tensor("agin", [g.SHARD], BF16).ap()
    agout = nc.dram_tensor("agout", [4 * g.SHARD], BF16).ap()
    qscr = nc.dram_tensor("q_scr", [D, TLOC], BF16).ap()
    oscr = nc.dram_tensor("o_scr", [NQS, 128, D], BF16).ap()
    xnscr = nc.dram_tensor("xn_scr", [D, TLOC], BF16).ap()
    rowscr = nc.dram_tensor("row_scr", [64], F32).ap()
    agin_st = nc.dram_tensor("agin_st", [64], BF16).ap()
    agout_st = nc.dram_tensor("agout_st", [256], BF16).ap()
    agout_k = agout.rearrange("(r c) -> r c", c=512)
    agout_v = agout.rearrange("(r c) -> r c", c=260)

    groups = [[0, 1, 2, 3], [4, 5, 6, 7]]
    sch = Sched(nc)

    with ExitStack() as stx:
        def sb(name, shape, dt):
            return stx.enter_context(nc.sbuf_tensor(name, shape, dt))

        x_in = [sb(f"x_in{d}", [128, TLOC], F32) for d in range(DCH)]
        xbf = [sb(f"xbf{d}", [128, HSF], BF16) for d in range(DCH)]
        params = sb("params_sb", [128, PCOLS], F32)
        brow = sb("brow_sb", [65, D], BF16)
        onesrow = sb("onesrow", [1, TC], BF16)
        ones65 = sb("ones65", [65, TC], BF16)
        idxtab = sb("idxtab_sb", [128, g.NIDX], U32)
        flagst = sb("flagst", [1, 512], BF16)
        mrowst = sb("mrowst", [1, 512], BF16)
        biasall = sb("biasall", [128, g.NBC], F32)
        rows = sb("rows_sb", [1, 256], F32)
        mask512 = [sb(f"mask_512_{i}", [128, 512], BF16) for i in range(4)]
        masks = {}
        mask_reg = {}
        for w in (128, 256, 512):
            for i in range(w // 128):
                j = i + (512 - w) // 128
                masks[(w, i)] = mask512[j][:, 0:w]
                mask_reg[(w, i)] = f"mask512_{j}"
        identb = sb("identb", [128, 128], BF16)
        identf = sb("identf", [128, 128], F32)
        blkind = [sb(f"blkind{c}", [128, 16], BF16) for c in range(DCH)]
        piota = sb("piota", [128, 1], F32)
        allones = sb("allones", [128, 128], BF16)
        sqt = [sb(f"sqt{i}", [128, TC], BF16) for i in range(2)]
        evac_t = sb("evac_t", [128, 4 * TC], BF16)
        ebuf_t = sb("ebuf_t", [128, 6 * 512], BF16)
        mu_bc = [sb(f"mu_bc{t}", [128, TC], BF16) for t in range(2)]
        r_bc = [sb(f"r_bc{t}", [128, TC], BF16) for t in range(2)]
        stat_ft = sb("stat_ft", [128, 3 * TC], F32)
        stat_f = [stat_ft[:, TC * i:TC * (i + 1)] for i in range(3)]
        evac = [evac_t[:, TC * i:TC * (i + 1)] for i in range(4)]
        vstage = sb("vstage", [128, 520], BF16)
        KBC = max(NBK * 128, 8 * 1024, 4 * TLOC)
        VBC = max(2 * NBK * 65, 8 * HSF, 8 * 1024)
        kbuf = [sb(f"kbuf{i}", [128, KBC], BF16) for i in range(2)]
        vbuf = [sb(f"vbuf{i}", [128, VBC], BF16) for i in range(2)]
        qpair = [sb(f"qpair{i}", [128, max(TLOC, 2 * D)], BF16)
                 for i in range(2)]
        ebuf = [ebuf_t[:, 512 * i:512 * (i + 1)] for i in range(6)]
        rcpb = [sb(f"rcp{i}", [128, 1], F32) for i in range(4)]
        otm = [sb(f"otm{i}", [128, 512], BF16) for i in range(2)]
        maxsq = [sb(f"maxsq{i}", [16, 1], F32) for i in range(2)]
        red16 = sb("red16", [16, 1], F32)
        col1 = [sb(f"col1_{i}", [128, 1], F32) for i in range(6)]
        wtile = [sb(f"wtile{i}", [128, 1024], BF16) for i in range(4)]
        if g.NBC >= 1024:
            w2half = [biasall[:, 0:1024].bitcast(BF16),
                      stat_ft[:, 0:1024].bitcast(BF16)]
        else:
            w2h_real = sb("w2h_real", [128, 2048], BF16)
            w2half = [w2h_real[:, :], stat_ft[:, 0:1024].bitcast(BF16)]
        wdtile = stat_ft[:, :].bitcast(BF16)
        wtile2v = [wdtile[:, 0:1024], wdtile[:, 1024:2048]]
        xtm = evac_t[:, 0:2 * D].bitcast(F32)
        xsq = ebuf_t[:, 0:2 * D].bitcast(F32)
        if g.NBC * 4 >= 4 * D:
            xtmf = biasall[:, 0:D].bitcast(F32) if False else biasall[:, 0:D]
        else:
            xtmf = sb("xtmf", [128, D], F32)[:, :]
        outtm = [qpair[i][:, 0:2 * D].bitcast(F32) for i in range(2)]
        PS = [stx.enter_context(nc.psum_tensor(f"ps{i}", [128, 512], F32))
              for i in range(7)]
        TRB = stx.enter_context(nc.psum_tensor("trb", [128, 1024], BF16))
        S0, S1, PA, PB, PC, PD, PM = range(7)
        psr = [f"ps{i}" for i in range(7)] + ["trb"]

        def DMA(engine, out, in_, reads, writes, tag="", slow=False):
            def f(eng):
                return eng.dma_start(out=out, in_=in_,
                                     allow_slow_non_contiguous=slow)
            return sch.add(engine, f, reads=reads, writes=writes, dma=True,
                           tag=tag)

        def MM(bank, psum_ap, lhsT, rhs, start, stop, reads, tag=""):
            def f(eng):
                return eng.matmul(psum_ap, lhsT, rhs, start=start, stop=stop)
            return sch.add("tensor", f, reads=reads, writes=[psr[bank]],
                           tag=tag)

        def TRANS(out_ap, in_ap, ident, reads, writes, tag=""):
            def f(eng):
                return eng.transpose(out_ap, in_ap, ident)
            return sch.add("tensor", f, reads=reads, writes=writes, tag=tag)

        def ACT(out, in_, func, reads, writes, bias=None, scale=None, tag=""):
            kw = {}
            if bias is not None:
                kw["bias"] = bias
            if scale is not None:
                kw["scale"] = scale

            def f(eng):
                return eng.activation(out, in_, func, **kw)
            return sch.add("scalar", f, reads=reads, writes=writes, tag=tag)

        def VOP(name, args, reads, writes, tag="", engine="vector"):
            def f(eng):
                return getattr(eng, name)(*args)
            return sch.add(engine, f, reads=reads, writes=writes, tag=tag)

        def GATHER(out, in_, idxcol, reads, writes, tag=""):
            def f(eng):
                return eng.indirect_dma_start(
                    out=out, out_offset=None, in_=in_,
                    in_offset=bass.IndirectOffsetOnAxis(idxcol, 0))
            return sch.add("gpsimd", f, reads=reads, writes=writes, dma=True,
                           tag=tag)

        def COLL(kind, op, ins, outs, reads, writes, tag="cc"):
            def f(eng):
                return eng.collective_compute(kind, op, replica_groups=groups,
                                              ins=ins, outs=outs)
            return sch.add("gpsimd", f, reads=reads, writes=writes, cc=True,
                           tag=tag)

        def kb_regions(i):
            return [f"kb{i}g{m}" for m in range((KBC // 128 + 3) // 4)]

        def vb_regions(i):
            return [f"vb{i}a{a}g{m}" for a in range(2)
                    for m in range((NBK + 3) // 4)]

        NFC1 = F // 128

        def h_slot(fc):
            per_kb = KBC // HSF
            per_vb = VBC // HSF
            if fc < per_kb:
                return kbuf[0], fc * HSF
            fc -= per_kb
            if fc < per_kb:
                return kbuf[1], fc * HSF
            fc -= per_kb
            if fc < per_vb:
                return vbuf[0], fc * HSF
            fc -= per_vb
            return vbuf[1], fc * HSF

        DMA("sync", params[:, :], params_h, ["params_h"], ["params"], "params")
        DMA("sync", idxtab[:, :], idx_h, ["idx_h"], ["idxtab"], "idxtab")
        off = 0
        for (w, i) in mc:
            j = i + (512 - w) // 128
            DMA("gpsimd", mask512[j][:, :], cpack_h[:, off:off + w],
                ["cpack_h"], [f"mask512_{j}"], f"mask512_{j}")
            off += w
        DMA("gpsimd", identb[:, :], cpack_h[:, off:off + 128],
            ["cpack_h"], ["identb"], "identb")
        DMA("sync", identf[:, :], cpack_h[:, off:off + 128],
            ["cpack_h"], ["identf"], "identf")
        off += 128
        for c in range(DCH):
            DMA("gpsimd", blkind[c][:, :], cpack_h[:, off:off + 16],
                ["cpack_h"], [f"blkind{c}"], f"blkind{c}")
            off += 16
        DMA("sync", piota[:, :], cpack_h[:, off:off + 1],
            ["cpack_h"], ["piota"], "piota", slow=True)
        VOP("memset", (onesrow[:, :], 1.0), [], ["onesrow"], "ones")
        VOP("memset", (allones[:, :], 1.0), [], ["allones"], "allones")
        VOP("memset", (ones65[:, :], 1.0), [], ["ones65"], "ones65")
        VOP("memset", (vstage[:, :], 1.0), [], ["vstage"], "vst_init")
        zview = kbuf[0][:, 0:4096]
        VOP("memset", (zview, 0.0), [], ["zbuf"], "zbuf")
        DMA("sync", bass.AP(agin_st.tensor, 0, [[1, 64]]),
            zview[0:1, 0:64], ["zbuf"], ["agin_s"], "zagst")
        ZCH = 128 * 4096
        nz = (g.SHARD + ZCH - 1) // ZCH
        for zi in range(nz):
            n = min(ZCH, g.SHARD - zi * ZCH)
            zrows = n // 4096
            rem = n - zrows * 4096
            if zrows:
                DMA("sync",
                    bass.AP(agin.tensor, zi * ZCH, [[4096, zrows], [1, 4096]]),
                    zview[0:zrows, :], ["zbuf"],
                    [f"aginb{b}" for b in range(8)] + ["agin_s"], f"zag{zi}")
            if rem:
                DMA("sync",
                    bass.AP(agin.tensor, zi * ZCH + zrows * 4096, [[1, rem]]),
                    zview[0:1, 0:rem], ["zbuf"],
                    [f"aginb{b}" for b in range(8)] + ["agin_s"], f"zagr{zi}")

        recD = 1.0 / D
        for q in range(NQS):
            DMA("sync", xtm, x_h[128 * q:128 * q + 128, :],
                ["x_h"], ["xtm"], f"xld{q}")
            ACT(xsq, xtm, AF.Square, ["xtm"], ["xsq"],
                tag=f"xsq{q}")
            VOP("reduce_sum", (col1[0][:, :], xtm, AX.X),
                ["xtm"], ["col0"], f"xsum{q}")
            VOP("reduce_sum", (col1[1][:, :], xsq, AX.X),
                ["xsq"], ["col1"], f"xss{q}")
            VOP("tensor_scalar", (col1[0][:, :], col1[0][:, :], recD, None,
                                  ALU.mult), ["col0"], ["col0"], f"mu{q}")
            VOP("tensor_scalar", (col1[1][:, :], col1[1][:, :], recD, None,
                                  ALU.mult), ["col1"], ["col1"], f"ms{q}")
            VOP("tensor_tensor", (col1[2][:, :], col1[0][:, :], col1[0][:, :],
                                  ALU.mult), ["col0"], ["col2"], f"mu2{q}")
            VOP("tensor_tensor", (col1[1][:, :], col1[1][:, :], col1[2][:, :],
                                  ALU.subtract), ["col1", "col2"], ["col1"],
                f"var{q}")
            VOP("tensor_scalar", (col1[1][:, :], col1[1][:, :], LN_EPS,
                                  None, ALU.add), ["col1"], ["col1"],
                f"veps{q}")
            ACT(col1[2][:, :], col1[1][:, :], AF.Sqrt, ["col1"], ["col2"],
                tag=f"sd{q}")
            VOP("reciprocal", (col1[3][:, :], col1[2][:, :]),
                ["col2"], ["col3"], f"rstd{q}")
            VOP("tensor_scalar", (xtm, xtm, col1[0][:, 0:1],
                                  col1[3][:, 0:1], ALU.subtract, ALU.mult),
                ["xtm", "col0", "col3"], ["xtm"], f"xn{q}")
            for dd in range(DCH):
                bank = PA + (dd % 2)
                TRANS(PS[bank][:, 0:128], xtm[:, 128 * dd:128 * dd + 128],
                      identf[:, :], ["xtm", "identf"], [psr[bank]],
                      f"xtr{q}_{dd}")
                VOP("tensor_copy",
                    (x_in[dd][:, 128 * q:128 * q + 128], PS[bank][:, 0:128]),
                    [psr[bank]], [f"x_in{dd}"], f"xev{q}_{dd}")
        for dd in range(DCH):
            DMA("gpsimd", xnscr[128 * dd:128 * dd + 128, :], x_in[dd][:, :],
                [f"x_in{dd}"], ["xnscr"], f"xnscr{dd}")

        def emit_ln(tagp, th, wcol=None, bcol=None):
            sch.barrier(["w2t0", "w2t1", "wu0", "wu1"],
                        ["stf0", "stf1", "stf2"])
            tof = HSF * th
            for dd in range(DCH):
                VOP("tensor_copy", (xbf[dd][:, :],
                                    x_in[dd][:, tof:tof + HSF]),
                    [f"x_in{dd}"], [f"xbf{dd}"], f"{tagp}cast{dd}")
            for t in range(NSUB):
                tc = slice(TC * t, TC * (t + 1))
                tb = t % 2
                for dd in range(DCH):
                    sq = sqt[dd % 2]
                    ACT(sq[:, :], xbf[dd][:, tc], AF.Square,
                        [f"xbf{dd}"], [f"sqt{dd % 2}"], tag=f"{tagp}sq{t}_{dd}")
                    MM(PM, PS[PM][:, :], allones[:, :], xbf[dd][:, tc],
                       dd == 0, dd == DCH - 1, ["allones", f"xbf{dd}"],
                       f"{tagp}sum{t}_{dd}")
                    MM(PA, PS[PA][:, :], allones[:, :], sq[:, :],
                       dd == 0, dd == DCH - 1, ["allones", f"sqt{dd % 2}"],
                       f"{tagp}ss{t}_{dd}")
                VOP("tensor_scalar", (mu_bc[tb][:, :], PS[PM][:, :], recD,
                                      None, ALU.mult), [psr[PM]],
                    [f"mu_bc{tb}"], f"{tagp}mu{t}")
                VOP("tensor_scalar", (stat_f[0], PS[PM][:, :], recD,
                                      None, ALU.mult), [psr[PM]], ["stf0"],
                    f"{tagp}muf{t}")
                VOP("tensor_scalar", (stat_f[1], PS[PA][:, :], recD,
                                      None, ALU.mult), [psr[PA]], ["stf1"],
                    f"{tagp}ssf{t}")
                VOP("tensor_tensor", (stat_f[2], stat_f[0],
                                      stat_f[0], ALU.mult),
                    ["stf0"], ["stf2"], f"{tagp}mu2{t}")
                VOP("tensor_tensor", (stat_f[1], stat_f[1],
                                      stat_f[2], ALU.subtract),
                    ["stf1", "stf2"], ["stf1"], f"{tagp}var{t}")
                VOP("tensor_scalar", (stat_f[1], stat_f[1],
                                      LN_EPS, None, ALU.add),
                    ["stf1"], ["stf1"], f"{tagp}veps{t}")
                ACT(stat_f[2], stat_f[1], AF.Sqrt,
                    ["stf1"], ["stf2"], tag=f"{tagp}sd{t}")
                VOP("reciprocal", (stat_f[0], stat_f[2]),
                    ["stf2"], ["stf0"], f"{tagp}rr{t}")
                VOP("tensor_copy", (r_bc[tb][:, :], stat_f[0]),
                    ["stf0"], [f"r_bc{tb}"], f"{tagp}rbc{t}")
            for dd in range(DCH):
                for t in range(NSUB):
                    tc = slice(TC * t, TC * (t + 1))
                    tb = t % 2
                    VOP("tensor_tensor", (xbf[dd][:, tc], xbf[dd][:, tc],
                                          mu_bc[tb][:, :], ALU.subtract),
                        [f"xbf{dd}", f"mu_bc{tb}"], [f"xbf{dd}"],
                        f"{tagp}ap1_{dd}_{t}")
                    VOP("tensor_tensor", (xbf[dd][:, tc], xbf[dd][:, tc],
                                          r_bc[tb][:, :], ALU.mult),
                        [f"xbf{dd}", f"r_bc{tb}"], [f"xbf{dd}"],
                        f"{tagp}ap2_{dd}_{t}")
                    if wcol is not None:
                        VOP("tensor_scalar",
                            (xbf[dd][:, tc], xbf[dd][:, tc],
                             params[:, wcol + dd:wcol + dd + 1],
                             params[:, bcol + dd:bcol + dd + 1],
                             ALU.mult, ALU.add),
                            [f"xbf{dd}", "params"], [f"xbf{dd}"],
                            f"{tagp}ap3_{dd}_{t}")


        def emit_qkv_weights(ly):
            sch.barrier(["w2t0", "w2t1", "wu0", "wu1"],
                        ["stf0", "stf1", "stf2", "biasall"])
            sch.barrier(kb_regions(0) + kb_regions(1) + vb_regions(0)
                        + vb_regions(1) + [f"hkb{i}" for i in range(NFC1)]
                        + ["zbuf", "xtm", "xsq", "w2t0", "w2t1", "wu0",
                           "wu1"]
                        + [f"ofm{d}" for d in range(DCH)],
                        [f"qkw{fc}" for fc in range(16)]
                        + [f"vw{dd}" for dd in range(DCH)])
            for fc in range(16):
                dst = kbuf[fc // 8][:, 1024 * (fc % 8):1024 * (fc % 8 + 1)]
                DMA("gpsimd",
                    dst.rearrange("p (c f) -> p c f", f=128),
                    wqkv_h[ly, :, 128 * fc:128 * fc + 128]
                    .rearrange("(c p) f -> p c f", p=128),
                    ["wqkv_h"], [f"qkw{fc}"], f"qkwld{ly}_{fc}")
            for dd in range(DCH):
                dst = vbuf[0][:, 1024 * dd:1024 * (dd + 1)]
                DMA("gpsimd", dst,
                    wqkv_h[ly, 128 * dd:128 * dd + 128, 2048:3072],
                    ["wqkv_h"], [f"vw{dd}"], f"vwld{ly}_{dd}")

        def emit_qkv_half(ly, th):
            tof = HSF * th
            qkbias0 = ly * PCL + 32
            if th == 0:
                for which in (0, 1):
                    VOP("memset", (maxsq[which][:, :], 0.0), [],
                        [f"maxsq{which}"], f"nmz{ly}_{which}")
            for fc in range(16):
                wt = kbuf[fc // 8][:, 1024 * (fc % 8):1024 * (fc % 8 + 1)]
                for t in range(NSUB):
                    bank = PB + ((fc if NSUB == 1 else t) % 2)
                    tc = slice(TC * t, TC * (t + 1))
                    gtc = slice(tof + TC * t, tof + TC * t + TC)
                    for dd in range(DCH):
                        MM(bank, PS[bank][:, :],
                           wt[:, 128 * dd:128 * dd + 128], xbf[dd][:, tc],
                           dd == 0, dd == DCH - 1,
                           [f"qkw{fc}", f"xbf{dd}"], f"qk{ly}_{th}_{fc}_{t}_{dd}")
                    ev = evac[t % 4]
                    ACT(ev, PS[bank][:, :], AF.Identity,
                        [psr[bank], "params"], [f"evac{t % 4}"],
                        bias=params[:, qkbias0 + fc:qkbias0 + fc + 1],
                        scale=1.0, tag=f"qkev{ly}_{th}_{fc}_{t}")
                    sq = sqt[(fc + t) % 2]
                    ACT(sq[:, :], ev, AF.Square, [f"evac{t % 4}"],
                        [f"sqt{(fc + t) % 2}"], tag=f"qnsq{ly}_{th}_{fc}_{t}")
                    nbank = PM if t == 0 else PD
                    MM(nbank, PS[nbank][0:16, :], blkind[fc % 8][:, :],
                       sq[:, :], fc % 8 == 0, fc % 8 == 7,
                       [f"blkind{fc % 8}", f"sqt{(fc + t) % 2}"],
                       f"qnmm{ly}_{th}_{fc}_{t}")
                    if fc % 8 == 7:
                        which = 0 if fc < 8 else 1
                        VOP("reduce_max", (red16[:, :], PS[nbank][0:16, :],
                                           AX.X), [psr[nbank]], ["red16"],
                            f"qnred{ly}_{th}_{which}_{t}")
                        VOP("tensor_tensor",
                            (maxsq[which][:, :], maxsq[which][:, :],
                             red16[:, :], ALU.max),
                            [f"maxsq{which}", "red16"], [f"maxsq{which}"],
                            f"qnmax{ly}_{th}_{which}_{t}")
                    if fc < 8:
                        DMA("sync", qscr[128 * fc:128 * fc + 128, gtc],
                            ev, [f"evac{t % 4}"], ["qscr"],
                            f"qst{ly}_{th}_{fc}_{t}")
                    else:
                        blk = fc - 8
                        ko = blk * g.PBLK + 2 * g.VSZ
                        dstap = bass.AP(agin.tensor, ko + tof + TC * t,
                                        [[TLOC, 128], [1, TC]])
                        DMA("sync", dstap, ev,
                            [f"evac{t % 4}"], [f"aginb{blk}"],
                            f"kst{ly}_{th}_{fc}_{t}")
            vb0 = ly * D
            for tt in range(tof // 128, (tof + HSF) // 128):
                ltt = tt - tof // 128
                for dd in range(DCH):
                    lhs = xbf[dd][:, 128 * ltt:128 * ltt + 128]
                    for half, bank in ((0, PA), (1, PC)):
                        MM(bank, PS[bank][:, :], lhs,
                           vbuf[0][:, 1024 * dd + 512 * half:
                                   1024 * dd + 512 * half + 512],
                           dd == 0, False,
                           [f"vw{dd}", f"xbf{dd}"],
                           f"v{ly}_{tt}_{dd}_{half}")
                MM(PA, PS[PA][:, :], onesrow[0:1, 0:128],
                   brow[0:1, 0:512], False, True,
                   ["onesrow", "brow0"], f"vb{ly}_{tt}_a")
                MM(PC, PS[PC][:, :], onesrow[0:1, 0:128],
                   brow[0:1, 512:1024], False, True,
                   ["onesrow", "brow0"], f"vb{ly}_{tt}_b")
                for half, bank in ((0, PA), (1, PC)):
                    VOP("tensor_copy",
                        (vstage[:, :].rearrange("p (h c) -> p h c", c=65)
                         [:, :, 0:64],
                         PS[bank][:, :].rearrange("p (h c) -> p h c", c=64)),
                        [psr[bank]], ["vstage"], f"vev{ly}_{tt}_{half}")
                    s_, j_ = tt // 4, tt % 4
                    for a_ in range(2):
                        dstap = bass.AP(
                            agin.tensor,
                            4 * half * g.PBLK + a_ * g.VSZ
                            + (512 * s_ + j_) * 65,
                            [[4 * 65, 128], [g.PBLK, 4], [1, 65]])
                        srcap = vstage[:, :].rearrange(
                            "p (b a c) -> p b a c", a=2, c=65)[:, :, a_, :]
                        DMA("sync", dstap, srcap,
                            ["vstage"],
                            [f"aginb{4 * half + bb}" for bb in range(4)],
                            f"vst{ly}_{tt}_{half}_{a_}")

        def emit_norms_ag(ly):
            DMA("gpsimd", bass.AP(agin_st.tensor, 0, [[1, 16]]),
                maxsq[1][:, :], ["maxsq1"], ["agin_s"], f"nst{ly}")
            COLL("AllGather", ALU.bypass,
                 [agin_st.rearrange("(a b) -> a b", b=64).opt()],
                 [agout_st.rearrange("(a b) -> a b", b=64).opt()],
                 ["agin_s"], ["agout_s"], f"agst{ly}")
            for pi, (ha, hb) in enumerate(g.pairs):
                blk = ha // 2
                ain = bass.AP(agin.tensor, blk * g.PBLK,
                              [[512, g.PBLK // 512], [1, 512]])
                aout = bass.AP(agout.tensor, blk * 4 * g.PBLK,
                               [[512, 4 * g.PBLK // 512], [1, 512]])
                COLL("AllGather", ALU.bypass, [ain.opt()], [aout.opt()],
                     [f"aginb{blk}"], [f"agoutb{blk}"], f"ag{ly}_{blk}")
            DMA("gpsimd", rowscr[0:16], maxsq[0][:, :], ["maxsq0"],
                ["rowscr"], f"nbq{ly}")
            DMA("sync", rows[0:1, 16:32], rowscr[0:16], ["rowscr"],
                ["rows"], f"nbq2{ly}")
            statrow = bass.AP(agout_st.tensor, 0,
                              [[1, 1], [1, 16], [64, 4]])

            def f_st(eng):
                return eng.dma_start(out=rows[0:1, 64:128].rearrange(
                    "a (h r) -> a h r", r=4), in_=statrow,
                    allow_slow_non_contiguous=True)
            sch.add("gpsimd", f_st, reads=["agout_s"], writes=["rows"],
                    dma=True, tag=f"nks{ly}")
            VOP("reduce_max", (rows[0:1, 0:16],
                               rows[0:1, 64:128].rearrange(
                                   "a (h r) -> a h r", r=4), AX.X),
                ["rows"], ["rows"], f"nkr{ly}")
            ACT(rows[0:1, 32:64], rows[0:1, 0:32], AF.Sqrt, ["rows"],
                ["rows"], tag=f"nsqrt{ly}")
            VOP("tensor_tensor", (rows[0:1, 48:64], rows[0:1, 32:48],
                                  rows[0:1, 48:64], ALU.mult),
                ["rows"], ["rows"], f"nmul{ly}")
            VOP("tensor_scalar", (rows[0:1, 48:64], rows[0:1, 48:64],
                                  0.125, M_MARGIN, ALU.mult, ALU.add),
                ["rows"], ["rows"], f"nmbar{ly}")
            hseg = {}
            for (hh, s, u, i), c in g.bias_col.items():
                a, b = hseg.get(hh, (c, c))
                hseg[hh] = (min(a, c), max(b, c))
            nchunk = (g.NBC + 511) // 512
            for cch in range(nchunk):
                c0 = 512 * cch
                cw = min(512, g.NBC - c0)
                DMA("gpsimd", flagst[0:1, 0:cw], flag_h[0:1, c0:c0 + cw],
                    ["flag_h"], ["flagst"], f"fstg{ly}_{cch}")
                for h in range(H):
                    a, b = hseg[h]
                    b += 1
                    lo, hi = max(a, c0), min(b, c0 + cw)
                    if lo >= hi:
                        continue
                    VOP("tensor_scalar",
                        (mrowst[0:1, lo - c0:hi - c0],
                         flagst[0:1, lo - c0:hi - c0],
                         KILL, rows[0:1, 48 + h:49 + h],
                         ALU.mult, ALU.add),
                        ["flagst", "rows"], ["mrowst"], f"mrow{ly}_{cch}_{h}")
                DMA("sync", stat_f[2][:, 0:cw], rampall_h[:, c0:c0 + cw],
                    ["rampall_h"], ["stf2"], f"rstg{ly}_{cch}")
                MM(PM, PS[PM][:, 0:cw], onesrow[0:1, 0:128],
                   mrowst[0:1, 0:cw], True, True,
                   ["onesrow", "mrowst"], f"bb{ly}_{cch}")
                VOP("tensor_tensor", (biasall[:, c0:c0 + cw],
                                      stat_f[2][:, 0:cw],
                                      PS[PM][:, 0:cw], ALU.subtract),
                    ["stf2", psr[PM]], ["biasall"], f"bs{ly}_{cch}")

        def emit_attention(ly):
            VW = NBK * 65
            for pi, (ha, hb) in enumerate(g.pairs):
                bi = pi % 2
                if pi < 2:
                    old = [f"qkw{fc}" for fc in range(16)] \
                        + [f"vw{dd}" for dd in range(DCH)]
                    sch.barrier(old, kb_regions(bi) + vb_regions(bi)
                                + [f"qp{bi}"])
                DMA("sync", qpair[bi][0:64, 0:TLOC],
                    qscr[64 * ha:64 * ha + 64, :], ["qscr"], [f"qp{bi}"],
                    f"qld{ly}_{pi}a")
                DMA("sync", qpair[bi][64:128, 0:TLOC],
                    qscr[64 * hb:64 * hb + 64, :], ["qscr"], [f"qp{bi}"],
                    f"qld{ly}_{pi}b")
                for (dt4, pidx) in g.kload2[ha // 2]:
                    m = dt4 // 4
                    GATHER(kbuf[bi][:, 128 * dt4:128 * dt4 + 512],
                           agout_k, idxtab[:, pidx:pidx + 1],
                           [f"agoutb{ha // 2}", "idxtab"],
                           [f"kb{bi}g{m}"],
                           f"kg{ly}_{pi}_{m}")
                for a, h in ((0, ha), (1, hb)):
                    for (dt4, pidx) in g.vload[h]:
                        m = dt4 // 4
                        GATHER(vbuf[bi][:, VW * a + 65 * dt4:
                                        VW * a + 65 * dt4 + 260],
                               agout_v, idxtab[:, pidx:pidx + 1],
                               [f"agoutb{h // 2}", "idxtab"],
                               [f"vb{bi}a{a}g{m}"],
                               f"vg{ly}_{pi}_{a}_{m}")
                erot = [0]
                for sch_s in range(NS):
                    for a, h in ((0, ha), (1, hb)):
                        W = g.Wq[h]
                        beta = W // 128
                        U = g.CH // W
                        for u in range(U):
                            n = g.nmat[(h, sch_s, u)]
                            qc = 512 * sch_s + W * u
                            rhs = qpair[bi][64 * a:64 * a + 64, qc:qc + W]
                            obanks = [PA, PB, PC, PD][:beta]
                            LAG = 3
                            epipe = []
                            for i in range(n + LAG):
                                if i < n:
                                    B = g.bslot(h, sch_s, u, i)
                                    col = g.bias_col[(h, sch_s, u, i)]
                                    sbank = [S0, S1, PM][i % 3]
                                    mk = f"kb{bi}g{B // 4}"
                                    MM(sbank, PS[sbank][:, 0:W],
                                       kbuf[bi][64 * a:64 * a + 64,
                                                128 * B:128 * B + 128],
                                       rhs, True, True,
                                       [mk, f"qp{bi}"],
                                       f"S{ly}_{pi}_{a}_{sch_s}_{u}_{i}")
                                    def f_dum(eng):
                                        return eng.matmul(
                                            TRB[:, 0:256].bitcast(F32),
                                            allones[:, 0:128],
                                            allones[:, 0:128],
                                            start=True, stop=True)
                                    sch.add("tensor", f_dum,
                                            reads=["allones"],
                                            writes=["trbd"], tag="dum")
                                    eb = ebuf[erot[0] % 6]
                                    ebr = f"eb{erot[0] % 6}"
                                    erot[0] += 1
                                    ACT(eb[:, 0:W], PS[sbank][:, 0:W],
                                        AF.Exp,
                                        [psr[sbank], "biasall"], [ebr],
                                        bias=biasall[:, col:col + 1],
                                        scale=0.125,
                                        tag=f"E{ly}_{pi}_{a}_{sch_s}_{u}_{i}")
                                    if i < beta:
                                        VOP("tensor_tensor",
                                            (eb[:, 0:W], eb[:, 0:W],
                                             masks[(W, i)], ALU.mult),
                                            [ebr, mask_reg[(W, i)]], [ebr],
                                            f"M{ly}_{pi}_{a}_{sch_s}_{u}_{i}")
                                    epipe.append((i, eb, ebr))
                                if i >= LAG:
                                    j, ejb, ejr = epipe[i - LAG]
                                    Bj = g.bslot(h, sch_s, u, j)
                                    mv = f"vb{bi}a{a}g{Bj // 4}"
                                    for qs in range(beta):
                                        MM(obanks[qs],
                                           PS[obanks[qs]][:, 0:65],
                                           ejb[:, 128 * qs:128 * qs + 128],
                                           vbuf[bi][:, VW * a + 65 * Bj:
                                                    VW * a + 65 * Bj + 65],
                                           j == 0, j == n - 1,
                                           [ejr, mv],
                                           f"PV{ly}_{pi}_{a}_{sch_s}_{u}_{j}_{qs}")
                            for qs in range(beta):
                                k_in_s = (W * u + 128 * qs) // 128
                                rc = rcpb[qs]
                                VOP("reciprocal",
                                    (rc[:, :], PS[obanks[qs]][:, 64:65]),
                                    [psr[obanks[qs]]], [f"rc{qs}"],
                                    f"R{ly}_{pi}_{a}_{sch_s}_{u}_{qs}")
                                VOP("tensor_scalar",
                                    (otm[sch_s % 2][:, 128 * k_in_s + 64 * a:
                                                    128 * k_in_s + 64 * a + 64],
                                     PS[obanks[qs]][:, 0:64], rc[:, 0:1],
                                     None, ALU.mult),
                                    [psr[obanks[qs]], f"rc{qs}"],
                                    [f"otm{sch_s % 2}"],
                                    f"ON{ly}_{pi}_{a}_{sch_s}_{u}_{qs}")
                    for a, h in ((0, ha), (1, hb)):
                        dstap = bass.AP(
                            oscr.tensor, (4 * sch_s) * 128 * D + 64 * h,
                            [[D, 128], [128 * D, 4], [1, 64]])
                        DMA("sync", dstap,
                            otm[sch_s % 2][:, :].rearrange(
                                "p (k c) -> p k c", c=128)
                            [:, :, 64 * a:64 * a + 64],
                            [f"otm{sch_s % 2}"], [f"oscr{sch_s}_{h}"],
                            f"ost{ly}_{pi}_{sch_s}_{a}")

        def emit_outproj(ly):
            sch.barrier(kb_regions(0) + kb_regions(1),
                        [f"ofm{d}" for d in range(DCH)])
            for q in range(NQS):
                rb = qpair[q % 2]
                if q < 2:
                    sch.barrier([f"qp{q % 2}"], [f"oin{q % 2}"])
                DMA("sync", rb[:, 0:D], oscr[q, :, :],
                    [f"oscr{q // 4}_{hh}" for hh in range(H)],
                    [f"oin{q % 2}"], f"old{ly}_{q}")
                for dd in range(DCH):
                    TRANS(TRB[:, 128 * dd:128 * dd + 128],
                          rb[:, 128 * dd:128 * dd + 128], identb[:, :],
                          [f"oin{q % 2}", "identb"], ["trb"],
                          f"otr{ly}_{q}_{dd}")
                    ofm = kbuf[dd // 4][:, TLOC * (dd % 4) + 128 * q:
                                       TLOC * (dd % 4) + 128 * q + 128]
                    VOP("tensor_copy",
                        (ofm, TRB[:, 128 * dd:128 * dd + 128]),
                        ["trb"], [f"ofm{dd}"], f"oev{ly}_{q}_{dd}")
            ob0 = ly * D
            for f in range(DCH):
                wt = wtile[f % 4]
                DMA("gpsimd",
                    wt[:, :].rearrange("p (c f) -> p c f", f=128),
                    wo_h[ly, :, 128 * f:128 * f + 128]
                    .rearrange("(c p) f -> p c f", p=128),
                    ["wo_h"], [f"wt{f % 4}"], f"wold{ly}_{f}")
                for t in range(NT):
                    bank = PA + (t % 2)
                    tc = slice(TC * t, TC * (t + 1))
                    for dd in range(DCH):
                        ofm = kbuf[dd // 4][:, TLOC * (dd % 4) + TC * t:
                                            TLOC * (dd % 4) + TC * t + TC]
                        MM(bank, PS[bank][:, :],
                           wt[:, 128 * dd:128 * dd + 128], ofm,
                           dd == 0, False,
                           [f"wt{f % 4}", f"ofm{dd}"], f"op{ly}_{f}_{t}_{dd}")
                    MM(bank, PS[bank][:, :], brow[32:33, 128 * f:
                                                  128 * f + 128],
                       ones65[32:33, :], False, True,
                       ["brow1", "ones65"], f"opb{ly}_{f}_{t}")
                    VOP("tensor_tensor",
                        (x_in[f][:, tc], x_in[f][:, tc], PS[bank][:, :],
                         ALU.add), [f"x_in{f}", psr[bank]], [f"x_in{f}"],
                        f"ores{ly}_{f}_{t}")

        def emit_ffn_half(ly, th):
            tof = HSF * th
            fb0 = ly * PCL + 48
            if th == 0:
                sch.barrier([f"ofm{d}" for d in range(DCH)]
                            + kb_regions(0) + kb_regions(1)
                            + vb_regions(0) + vb_regions(1),
                            [f"hkb{i}" for i in range(NFC1)]
                            + ["w2t0", "w2t1"])
            for fc in range(NFC1):
                wt = wtile[fc % 4]
                DMA("gpsimd",
                    wt[:, :].rearrange("p (c f) -> p c f", f=128),
                    w1_h[ly, :, 128 * fc:128 * fc + 128]
                    .rearrange("(c p) f -> p c f", p=128),
                    ["w1_h"], [f"wt{fc % 4}"], f"w1ld{ly}_{th}_{fc}")
                hb, hcol = h_slot(fc)
                for sub in range(NSUB):
                    bank = PA + ((fc if NSUB == 1 else sub) % 2)
                    tc = slice(TC * sub, TC * sub + TC)
                    for dd in range(DCH):
                        MM(bank, PS[bank][:, :],
                           wt[:, 128 * dd:128 * dd + 128],
                           xbf[dd][:, tc], dd == 0, dd == DCH - 1,
                           [f"wt{fc % 4}", f"xbf{dd}"],
                           f"f1_{ly}_{th}_{fc}_{sub}_{dd}")
                    ACT(hb[:, hcol + TC * sub:hcol + TC * sub + TC],
                        PS[bank][:, :], AF.Gelu,
                        [psr[bank], "params"], [f"hkb{fc}"],
                        bias=params[:, fb0 + fc:fb0 + fc + 1], scale=1.0,
                        tag=f"f1e{ly}_{th}_{fc}_{sub}")
            sch.barrier(["stf0", "stf1", "stf2", "biasall"],
                        ["w2t0", "w2t1"])
            for f in range(DCH):
                for hh in range(2):
                    w2t = w2half[hh]
                    DMA("gpsimd",
                        w2t.rearrange("p (c f) -> p c f", f=128),
                        w2_h[ly, 2048 * hh:2048 * (hh + 1),
                             128 * f:128 * f + 128]
                        .rearrange("(c p) f -> p c f", p=128),
                        ["w2_h"], [f"w2t{hh}"], f"w2ld{ly}_{th}_{f}_{hh}")
                for sub in range(NSUB):
                    bank = PC + ((f if NSUB == 1 else sub) % 2)
                    tc = slice(TC * sub, TC * sub + TC)
                    for hk in range(NFC1):
                        hb, hcol = h_slot(hk)
                        w2t = w2half[hk // 16]
                        MM(bank, PS[bank][:, :],
                           w2t[:, 128 * (hk % 16):128 * (hk % 16) + 128],
                           hb[:, hcol + TC * sub:hcol + TC * sub + TC],
                           hk == 0, False,
                           [f"w2t{hk // 16}", f"hkb{hk}"],
                           f"f2_{ly}_{th}_{f}_{sub}_{hk}")
                    MM(bank, PS[bank][:, :],
                       brow[64:65, 128 * f:128 * f + 128],
                       ones65[64:65, :], False, True,
                       ["brow2", "ones65"], f"f2b{ly}_{th}_{f}_{sub}")
                    VOP("tensor_tensor",
                        (x_in[f][:, tof + TC * sub:tof + TC * sub + TC],
                         x_in[f][:, tof + TC * sub:tof + TC * sub + TC],
                         PS[bank][:, :], ALU.add),
                        [f"x_in{f}", psr[bank]], [f"x_in{f}"],
                        f"f2r{ly}_{th}_{f}_{sub}")

        def emit_brow(ly):
            for rbase, rowi in ((0, 0), (32, 1), (64, 2)):
                DMA("gpsimd", brow[rbase:rbase + 1, :],
                    brow_h[rowi:rowi + 1, ly * D:(ly + 1) * D],
                    ["brow_h"], [f"brow{rowi}"], f"brow{ly}_{rowi}")

        def emit_final():
            fincol = L * PCL
            NFCG = FG // 128
            per_kb = KBC // HSF

            def hg_loc(fc, sub):
                if fc < per_kb:
                    hb, hcol = kbuf[0], fc * HSF
                elif fc < 2 * per_kb:
                    hb, hcol = kbuf[1], (fc - per_kb) * HSF
                else:
                    hb, hcol = vbuf[0], (fc - 2 * per_kb) * HSF
                return hb[:, hcol + TC * sub:hcol + TC * sub + TC], \
                    f"hkb{fc}"

            sch.barrier(sum((vb_regions(i) for i in range(2)), [])
                        + kb_regions(0) + kb_regions(1)
                        + [f"hkb{i}" for i in range(NFC1)]
                        + [f"qp{i}" for i in range(2)] + ["biasall"]
                        + ["w2t0", "w2t1"]
                        + [f"ofm{d}" for d in range(DCH)],
                        [f"xng{d}" for d in range(DCH)]
                        + [f"hkb{i}" for i in range(NFCG)]
                        + ["xtmf", "wu0", "wu1"])
            for th in range(NTH):
                tof = HSF * th
                emit_ln("fin%d" % th, th, wcol=fincol, bcol=fincol + 8)
                xng = []
                for dd in range(DCH):
                    ap = vbuf[1][:, HSF * dd:HSF * dd + HSF]
                    xng.append(ap)
                    DMA("sync", ap,
                        xnscr[128 * dd:128 * dd + 128, tof:tof + HSF],
                        ["xnscr"], [f"xng{dd}"], f"gxn{th}_{dd}")
                sch.barrier(["w2t0", "w2t1", "stf0", "stf1", "stf2"], ["wu0", "wu1"])
                for fc in range(NFCG):
                    wtg = wtile[fc % 4]
                    wtu = wtile2v[fc % 2]
                    DMA("gpsimd",
                        wtg[:, :].rearrange("p (c f) -> p c f", f=128),
                        wg_h[:, 128 * fc:128 * fc + 128]
                        .rearrange("(c p) f -> p c f", p=128),
                        ["wg_h"], [f"wt{fc % 4}"], f"wgld{th}_{fc}")
                    DMA("gpsimd",
                        wtu.rearrange("p (c f) -> p c f", f=128),
                        wu_h[:, 128 * fc:128 * fc + 128]
                        .rearrange("(c p) f -> p c f", p=128),
                        ["wu_h"], [f"wu{fc % 2}"], f"wuld{th}_{fc}")
                    for sub in range(NSUB):
                        rot = (fc if NSUB == 1 else sub) % 2
                        bg = PA + rot
                        bu = PC + rot
                        for dd in range(DCH):
                            MM(bg, PS[bg][:, :],
                               wtg[:, 128 * dd:128 * dd + 128],
                               xng[dd][:, TC * sub:TC * sub + TC],
                               dd == 0, dd == DCH - 1,
                               [f"wt{fc % 4}", f"xng{dd}"],
                               f"gg_{th}_{fc}_{sub}_{dd}")
                            MM(bu, PS[bu][:, :],
                               wtu[:, 128 * dd:128 * dd + 128],
                               xng[dd][:, TC * sub:TC * sub + TC],
                               dd == 0, dd == DCH - 1,
                               [f"wu{fc % 2}", f"xng{dd}"],
                               f"gu_{th}_{fc}_{sub}_{dd}")
                        sil = sqt[rot]
                        ACT(sil[:, :], PS[bg][:, :], AF.Silu,
                            [psr[bg]], [f"sqt{rot}"], scale=DEQ,
                            tag=f"gsil{th}_{fc}_{sub}")
                        upv = evac[rot]
                        VOP("tensor_copy", (upv, PS[bu][:, :]),
                            [psr[bu]], [f"evac{rot}"],
                            f"gup{th}_{fc}_{sub}")
                        hap, hreg = hg_loc(fc, sub)
                        VOP("tensor_tensor", (hap, sil[:, :], upv,
                                              ALU.mult),
                            [f"sqt{rot}", f"evac{rot}"], [hreg],
                            f"gh{th}_{fc}_{sub}")
                sch.barrier(["wu0", "wu1"], ["w2t0"])
                for f in range(DCH):
                    DMA("gpsimd",
                        wdtile[:, 0:128 * NFCG]
                        .rearrange("p (c f) -> p c f", f=128),
                        wd_h[:, 128 * f:128 * f + 128]
                        .rearrange("(c p) f -> p c f", p=128),
                        ["wd_h"], ["w2t0"], f"wdld{th}_{f}")
                    for sub in range(NSUB):
                        rot = (f if NSUB == 1 else sub) % 2
                        bank = PA + rot
                        for hk in range(NFCG):
                            hap, hreg = hg_loc(hk, sub)
                            MM(bank, PS[bank][:, :],
                               wdtile[:, 128 * hk:128 * hk + 128],
                               hap, hk == 0, hk == NFCG - 1,
                               ["w2t0", hreg],
                               f"gd{th}_{f}_{sub}_{hk}")
                        gt = evac[2 + rot]
                        ACT(gt, PS[bank][:, :], AF.Silu,
                            [psr[bank]], [f"evac{2 + rot}"],
                            scale=DEQ * DEQ, tag=f"gout{th}_{f}_{sub}")
                        tc = slice(tof + TC * sub, tof + TC * sub + TC)
                        ltc = slice(TC * sub, TC * sub + TC)
                        VOP("tensor_tensor",
                            (x_in[f][:, tc], xbf[f][:, ltc], gt,
                             ALU.mult),
                            [f"xbf{f}", f"evac{2 + rot}"], [f"x_in{f}"],
                            f"gc{th}_{f}_{sub}")
                for q in range(tof // 128, (tof + HSF) // 128):
                    DMA("sync", xtmf, x_h[128 * q:128 * q + 128, :],
                        ["x_h"], ["xtmf"], f"fxld{q}")
                    for dd in range(DCH):
                        bank = PA if dd < 4 else PB
                        TRANS(PS[bank][:, 128 * (dd % 4):
                                       128 * (dd % 4) + 128],
                              x_in[dd][:, 128 * q:128 * q + 128],
                              identf[:, :],
                              [f"x_in{dd}", "identf"], [psr[bank]],
                              f"ftr{q}_{dd}")
                    ot = outtm[q % 2]
                    VOP("tensor_tensor", (ot[:, 0:512], PS[PA][:, :],
                                          xtmf[:, 0:512], ALU.add),
                        [psr[PA], "xtmf"], [f"outtm{q % 2}"], f"fc0_{q}")
                    VOP("tensor_tensor", (ot[:, 512:1024], PS[PB][:, :],
                                          xtmf[:, 512:1024], ALU.add),
                        [psr[PB], "xtmf"], [f"outtm{q % 2}"], f"fc1_{q}")
                    DMA("sync", out_h[128 * q:128 * q + 128, :], ot[:, :],
                        [f"outtm{q % 2}"], ["out_h"], f"fst{q}")

        for ly in range(L):
            lc = ly * PCL
            emit_brow(ly)
            emit_qkv_weights(ly)
            for th in range(NTH):
                emit_ln(f"l{ly}a{th}", th, wcol=lc, bcol=lc + 8)
                emit_qkv_half(ly, th)
            emit_norms_ag(ly)
            emit_attention(ly)
            emit_outproj(ly)
            for th in range(NTH):
                emit_ln(f"l{ly}b{th}", th, wcol=lc + 16, bcol=lc + 24)
                emit_ffn_half(ly, th)
        sch.barrier([f"qp{i}" for i in range(2)] + ["oin0", "oin1"],
                    ["outtm0", "outtm1"])
        emit_final()
        sch.finalize(final_engine="sync")
    return nc


def _pack_params(g, ins):
    L, F = g.L, g.F
    PCL = 32 + 16 + F // 128
    cols = []

    def chunked(v):
        return np.ascontiguousarray(v.reshape(-1, 128).T)

    out = np.zeros((128, L * PCL + 16), dtype=np.float32)
    for ly in range(L):
        c = ly * PCL
        out[:, c:c + 8] = chunked(ins["L0_attn_ln1_w"][ly])
        out[:, c + 8:c + 16] = chunked(ins["L0_attn_ln1_b"][ly])
        out[:, c + 16:c + 24] = chunked(ins["L0_attn_ln2_w"][ly])
        out[:, c + 24:c + 32] = chunked(ins["L0_attn_ln2_b"][ly])
        out[:, c + 32:c + 48] = chunked(ins["L0_attn_in_proj_b"][ly][:2048])
        out[:, c + 48:c + 48 + F // 128] = chunked(ins["L0_attn_ffn1_b"][ly])
    out[:, L * PCL:L * PCL + 8] = chunked(ins["L0_attn_final_w"])
    out[:, L * PCL + 8:L * PCL + 16] = chunked(ins["L0_attn_final_b"])
    return out


def _host_tables(g, gi):
    flag = np.zeros((1, g.NBC), dtype=np.float32)
    for (h, s, u, i), c in g.bias_col.items():
        beta = g.Wq[h] // 128
        t = (4 * s + gi) * 4 + (u + 1) * beta - 1 - i
        if t < 0:
            flag[0, c] = 1.0
    idx = np.zeros((128, g.NIDX), dtype=np.uint32)
    for p, (kind, hb_, m) in enumerate(g.idx_plan):
        if kind == "K2":
            for a in range(2):
                h = 2 * hb_ + a
                cg = m + (g.bmin[h] - g.padlo[h]) // 4 + gi
                cg = min(max(cg, 0), g.NCH - 1)
                r, sp = cg % 4, cg // 4
                rbase = hb_ * 4 * g.PBLK + r * g.PBLK
                base = (rbase + 2 * g.VSZ + a * g.KSZ) // 512
                idx[64 * a:64 * a + 64, p] = \
                    base + np.arange(64) * (g.TLOC // 512) + sp
        else:
            h = hb_
            cg = m + (g.bmin[h] - g.padlo[h]) // 4 + gi
            cg = min(max(cg, 0), g.NCH - 1)
            r, sp = cg % 4, cg // 4
            rbase = (h // 2) * 4 * g.PBLK + r * g.PBLK
            base = (rbase + (h % 2) * g.VSZ) // 260
            idx[:, p] = base + 128 * sp + np.arange(128)
    return flag, idx


_GEOM = None
_NC = None


def prepare_in_maps(g, ins):
    x = np.asarray(ins["L0_attn_x"], dtype=np.float32)
    wqkv = np.ascontiguousarray(
        np.asarray(ins["L0_attn_in_proj_w"], np.float32).transpose(0, 2, 1))
    wo = np.ascontiguousarray(
        np.asarray(ins["L0_attn_out_proj_w"], np.float32).transpose(0, 2, 1))
    w1 = np.ascontiguousarray(
        np.asarray(ins["L0_attn_ffn1_w"], np.float32).transpose(0, 2, 1))
    w2 = np.ascontiguousarray(
        np.asarray(ins["L0_attn_ffn2_w"], np.float32).transpose(0, 2, 1))
    wg = np.ascontiguousarray(np.asarray(ins["L0_ffn_w_gate"], np.int32).T)
    wu = np.ascontiguousarray(np.asarray(ins["L0_ffn_w_up"], np.int32).T)
    wd = np.ascontiguousarray(np.asarray(ins["L0_ffn_w_down"], np.int32).T)
    params = _pack_params(g, ins)
    brow = np.stack([
        np.ascontiguousarray(
            np.asarray(ins["L0_attn_in_proj_b"], np.float32)[:, 2048:3072]
        ).reshape(-1),
        np.asarray(ins["L0_attn_out_proj_b"], np.float32).reshape(-1),
        np.asarray(ins["L0_attn_ffn2_b"], np.float32).reshape(-1),
    ]).astype(np.float32)
    cpack, rampall = host_consts(g)
    in_maps = []
    for c in range(8):
        seq, gi = c // 4, c % 4
        slab = np.ascontiguousarray(np.concatenate(
            [x[seq, (4 * s + gi) * g.CH:(4 * s + gi + 1) * g.CH]
             for s in range(g.NS)], axis=0))
        flag, idx = _host_tables(g, gi)
        in_maps.append({
            "x_slab": slab, "wqkv": wqkv, "wo": wo, "w1": w1, "w2": w2,
            "wg": wg, "wu": wu, "wd": wd, "params": params, "brow": brow,
            "flagrow": flag, "idxtab": idx, "cpack": cpack,
            "rampall": rampall,
        })
    return in_maps


def assemble_out(g, slabs):
    out = np.zeros((2, g.T, g.D), dtype=np.float32)
    for c in range(8):
        seq, gi = c // 4, c % 4
        for s in range(g.NS):
            out[seq, (4 * s + gi) * g.CH:(4 * s + gi + 1) * g.CH] = \
                slabs[c][s * g.CH:(s + 1) * g.CH]
    return out


def kernel(**ins):
    global _GEOM, _NC
    g = _GEOM or Geom(T=8192, depth=4)
    _GEOM = g
    if _NC is None:
        _NC = build_program(g)
    in_maps = prepare_in_maps(g, ins)
    res = run_bass_kernel_spmd(_NC, in_maps, core_ids=list(range(8)))
    return assemble_out(g, [res.results[c]["out_slab"] for c in range(8)])


# version 15
# speedup vs baseline: 1.0316x; 1.0316x over previous
import math
from contextlib import ExitStack

import numpy as np
import concourse.bass as bass
import concourse.mybir as mybir
from concourse.bass_utils import run_bass_kernel_spmd

F32 = mybir.dt.float32
BF16 = mybir.dt.bfloat16
I32 = mybir.dt.int32
U32 = mybir.dt.uint32
AF = mybir.ActivationFunctionType
ALU = mybir.AluOpType
AX = mybir.AxisListType

LN_EPS = 1e-5
DEQ = 1.0 / 127.0
M_MARGIN = 3.0
SKIP_MARGIN = 92.0
WQ_CAP = 64.0
KILL = 30000.0


def alibi_slopes(h):
    closest = 2 ** math.floor(math.log2(h))
    base = 2 ** (-(2 ** -(math.log2(closest) - 3)))
    slopes = base ** np.arange(1, closest + 1)
    if closest != h:
        eb = 2 ** (-(2 ** -(math.log2(2 * closest) - 3)))
        extra = eb ** np.arange(1, 2 * (h - closest) + 1, 2)
        slopes = np.concatenate([slopes, extra])
    return slopes.astype(np.float64)


class Geom:
    def __init__(self, T=8192, depth=4):
        self.B, self.T, self.D, self.H, self.HD = 2, T, 1024, 16, 64
        self.L, self.F, self.FG = depth, 4096, 3072
        self.NC, self.GRP = 8, 4
        self.CH = 512
        assert T % (4 * self.CH) == 0
        self.NCH = T // self.CH
        self.NS = self.NCH // 4
        self.TLOC = self.NS * self.CH
        self.NKT = T // 128
        self.NQS = self.TLOC // 128
        self.DCH = self.D // 128
        self.TC = 512
        self.NT = self.TLOC // self.TC
        self.slopes = alibi_slopes(self.H)
        self.Wq = []
        for sl in self.slopes:
            w = 128
            while w < min(512, self.CH) and sl * (2 * w - 1) <= WQ_CAP:
                w *= 2
            self.Wq.append(w)
        self.nslot = [
            min(self.NKT, int((SKIP_MARGIN / sl + w - 1) // 128) + 1)
            for sl, w in zip(self.slopes, self.Wq)
        ]
        self.nmat = {}
        for h in range(self.H):
            w = self.Wq[h]
            for s in range(self.NS):
                for u in range(self.CH // w):
                    caus = ((4 * s + 3) * self.CH + (u + 1) * w) // 128
                    self.nmat[(h, s, u)] = min(self.nslot[h], caus)
        self.bmin, self.nbuf = {}, {}
        for h in range(self.H):
            beta = self.Wq[h] // 128
            bm, bx = 10 ** 9, -10 ** 9
            for s in range(self.NS):
                w = self.Wq[h]
                for u in range(self.CH // w):
                    n = self.nmat[(h, s, u)]
                    bm = min(bm, 16 * s + (u + 1) * (w // 128) - n)
                    bx = max(bx, 16 * s + (u + 1) * (w // 128) - 1)
            self.bmin[h] = bm
            self.nbuf[h] = bx - bm + 1
        border = sorted(range(self.H // 2),
                        key=lambda b: (-max(self.nbuf[2 * b],
                                            self.nbuf[2 * b + 1]), b))
        self.pairs = [(2 * b, 2 * b + 1) for b in border]
        self.bias_col = {}
        nb = 0
        for h in range(self.H):
            w = self.Wq[h]
            for s in range(self.NS):
                for u in range(self.CH // w):
                    for i in range(self.nmat[(h, s, u)]):
                        self.bias_col[(h, s, u, i)] = nb
                        nb += 1
        self.NBC = nb
        for (ha, hb) in self.pairs:
            assert ha // 2 == hb // 2, "pair must map to one AG block"
        self.VROW = 65
        self.VSZ = self.TLOC * self.VROW
        self.KSZ = 64 * self.TLOC
        lcm = 33280
        raw = 2 * (self.VSZ + self.KSZ)
        self.PBLK = ((raw + lcm - 1) // lcm) * lcm
        self.SHARD = 8 * self.PBLK
        assert self.PBLK % 512 == 0 and self.PBLK % 260 == 0
        assert self.VSZ % 260 == 0 and (2 * self.VSZ) % 512 == 0
        assert self.KSZ % 512 == 0
        self.padlo = {}
        self.kload = {}
        self.idx_plan = []
        for h in range(self.H):
            al = (-self.bmin[h]) % 4
            self.padlo[h] = (4 - al) % 4 if al != 0 else 0
            nb_tiles = self.nbuf[h] + self.padlo[h]
            ngrp = (nb_tiles + 3) // 4
            self.kload[h] = []
            for m in range(ngrp):
                self.idx_plan.append(("K", h, m))
                self.kload[h].append((4 * m, len(self.idx_plan) - 1))
            self.vload_base = None
        self.vload = {}
        for h in range(self.H):
            ngrp = (self.nbuf[h] + self.padlo[h] + 3) // 4
            self.vload[h] = []
            for m in range(ngrp):
                self.idx_plan.append(("V", h, m))
                self.vload[h].append((4 * m, len(self.idx_plan) - 1))
        self.NIDX = len(self.idx_plan)
        self.NBK = max(4 * ((self.nbuf[h] + self.padlo[h] + 3) // 4)
                       for h in range(self.H))

    def bslot(self, h, s, u, i):
        beta = self.Wq[h] // 128
        return 16 * s + (u + 1) * beta - 1 - i - self.bmin[h] + self.padlo[h]

    def mask_cols(self):
        return [(512, i) for i in range(4)]


def host_consts(g):
    mc = g.mask_cols()
    CC = sum(w for w, i in mc) + 128 + 8 * 16 + 1
    cp = np.zeros((128, CC), dtype=np.float32)
    p = np.arange(128)
    off = 0
    for (w, i) in mc:
        q = np.arange(w)[None, :]
        cp[:, off:off + w] = (q >= (p[:, None] + w - 128 * (i + 1)))
        off += w
    cp[:, off:off + 128] = np.eye(128, dtype=np.float32)
    off += 128
    for c in range(8):
        blk = np.zeros((128, 16), dtype=np.float32)
        for pp in range(128):
            blk[pp, 2 * c + (1 if pp >= 64 else 0)] = 1.0
        cp[:, off + 16 * c: off + 16 * (c + 1)] = blk
    off += 128
    cp[:, off] = p
    ramp = np.zeros((128, g.NBC), dtype=np.float32)
    for (h, s, u, i), col in g.bias_col.items():
        w = g.Wq[h]
        ramp[:, col] = g.slopes[h] * (p + w - 128 * (i + 1))
    return cp, ramp


class _Op:
    __slots__ = ("idx", "engine", "emit", "reads", "writes", "deps",
                 "need_inc", "inc_value", "inc_amt", "is_dma", "queue", "tag")

    def __init__(self, idx, engine, emit, reads, writes, is_dma, queue, tag,
                 inc_amt):
        self.idx, self.engine, self.emit = idx, engine, emit
        self.reads, self.writes = reads, writes
        self.deps = set()
        self.need_inc, self.inc_value = False, None
        self.inc_amt = inc_amt
        self.is_dma, self.queue, self.tag = is_dma, queue, tag


_COMPUTE = ("tensor", "vector", "scalar", "gpsimd")
_QUEUES = ("syncq", "gpsq", "scalq", "ccq")
_QOF = {"sync": "syncq", "gpsimd": "gpsq", "scalar": "scalq"}


class Sched:
    def __init__(self, nc):
        self.nc = nc
        self.ops = []
        self.last_writer = {}
        self.readers = {}
        self.pending = {}

    def add(self, engine, emit, reads=(), writes=(), dma=False, cc=False,
            tag=""):
        queue = "ccq" if cc else (_QOF[engine] if dma else None)
        op = _Op(len(self.ops), engine, emit, list(reads), list(writes),
                 dma or cc, queue, tag, 1 if cc else (16 if dma else 1))
        for r in op.reads:
            w = self.last_writer.get(r)
            if w is not None:
                op.deps.add(w)
        for w in op.writes:
            pw = self.last_writer.get(w)
            if pw is not None:
                op.deps.add(pw)
            for rd in self.readers.get(w, ()):
                op.deps.add(rd)
            if w in self.pending:
                op.deps |= self.pending.pop(w)
        op.deps.discard(op.idx)
        self.ops.append(op)
        for r in op.reads:
            self.readers.setdefault(r, []).append(op.idx)
        for w in op.writes:
            self.last_writer[w] = op.idx
            self.readers[w] = []
        return op.idx

    def barrier(self, old_regions, new_regions):
        dep = set()
        for r in old_regions:
            if r in self.last_writer:
                dep.add(self.last_writer[r])
            dep.update(self.readers.get(r, ()))
        for r in new_regions:
            self.pending.setdefault(r, set()).update(dep)

    DMA_LANES = 8

    def finalize(self, final_engine="sync"):
        nc, ops = self.nc, self.ops
        for op in ops:
            for d in op.deps:
                ops[d].need_inc = True
        for op in ops:
            if op.is_dma:
                op.need_inc = True
        lane_count = {}
        for op in ops:
            if op.is_dma and op.queue != "ccq":
                j = lane_count.get(op.queue, 0)
                lane_count[op.queue] = j + 1
                op.queue = f"{op.queue}{j % self.DMA_LANES}"
        counters = {}
        for op in ops:
            dom = op.queue if op.is_dma else op.engine
            if op.need_inc:
                counters[dom] = counters.get(dom, 0) + op.inc_amt
                op.inc_value = counters[dom]
        self.final_counts = dict(counters)

        with ExitStack() as st:
            sem_names = list(_COMPUTE) + ["ccq"] + [
                f"{q}{r}" for q in ("syncq", "gpsq", "scalq")
                for r in range(self.DMA_LANES)]
            sems = {n: st.enter_context(nc.semaphore("sem_" + n))
                    for n in sem_names}
            block = st.enter_context(nc.Block())
            by_eng = {e: [] for e in
                      ("tensor", "vector", "scalar", "gpsimd", "sync")}
            for op in ops:
                by_eng[op.engine].append(op)
            wm = {}

            def body_for(ename, elist):
                def body(eng):
                    for op in elist:
                        tgt = {}
                        for d in op.deps:
                            p = ops[d]
                            dom = p.queue if p.is_dma else p.engine
                            if (not p.is_dma) and p.engine == op.engine \
                                    and p.engine == "tensor":
                                continue
                            v = p.inc_value
                            if wm.get((ename, dom), 0) >= v:
                                continue
                            tgt[dom] = max(tgt.get(dom, 0), v)
                        if op.is_dma and op.queue != "ccq" \
                                and op.inc_value > 16:
                            dom = op.queue
                            v = op.inc_value - 16
                            if wm.get((ename, dom), 0) < v:
                                tgt[dom] = max(tgt.get(dom, 0), v)
                        for dom in sorted(tgt):
                            eng.wait_ge(sems[dom], tgt[dom])
                            wm[(ename, dom)] = tgt[dom]
                        inst = op.emit(eng)
                        if op.need_inc:
                            dom = op.queue if op.is_dma else op.engine
                            assert inst is not None, f"op needs inst: {op.tag}"
                            inst.then_inc(sems[dom], op.inc_amt)
                    if ename == final_engine:
                        for dom in sorted(counters):
                            eng.wait_ge(sems[dom], counters[dom])
                return body

            for ename in ("sync", "tensor", "vector", "scalar", "gpsimd"):
                getattr(block, ename)(body_for(ename, by_eng[ename]))


def build_program(g: Geom):
    nc = bass.Bass("TRN2", target_bir_lowering=False, debug=False)
    D, H, L, F, FG = g.D, g.H, g.L, g.F, g.FG
    TLOC, TC, NT, NS, DCH, NQS = g.TLOC, g.TC, g.NT, g.NS, g.DCH, g.NQS
    NBK = g.NBK
    HSF = min(1024, TLOC)
    NTH = TLOC // HSF
    NSUB = HSF // TC

    x_h = nc.dram_tensor("x_slab", [TLOC, D], F32, kind="ExternalInput").ap()
    wqkv_h = nc.dram_tensor("wqkv", [L, D, 3 * D], F32, kind="ExternalInput").ap()
    wo_h = nc.dram_tensor("wo", [L, D, D], F32, kind="ExternalInput").ap()
    w1_h = nc.dram_tensor("w1", [L, D, F], F32, kind="ExternalInput").ap()
    w2_h = nc.dram_tensor("w2", [L, F, D], F32, kind="ExternalInput").ap()
    wg_h = nc.dram_tensor("wg", [D, FG], I32, kind="ExternalInput").ap()
    wu_h = nc.dram_tensor("wu", [D, FG], I32, kind="ExternalInput").ap()
    wd_h = nc.dram_tensor("wd", [FG, D], I32, kind="ExternalInput").ap()
    PCL = 32 + 16 + F // 128
    PCOLS = L * PCL + 16
    params_h = nc.dram_tensor("params", [128, PCOLS], F32,
                              kind="ExternalInput").ap()
    brow_h = nc.dram_tensor("brow", [3, L * D], F32, kind="ExternalInput").ap()
    flag_h = nc.dram_tensor("flagrow", [1, g.NBC], F32,
                            kind="ExternalInput").ap()
    idx_h = nc.dram_tensor("idxtab", [128, g.NIDX], U32,
                           kind="ExternalInput").ap()
    mc = g.mask_cols()
    CC = sum(w for w, i in mc) + 128 + 8 * 16 + 1
    cpack_h = nc.dram_tensor("cpack", [128, CC], F32,
                             kind="ExternalInput").ap()
    rampall_h = nc.dram_tensor("rampall", [128, g.NBC], F32,
                               kind="ExternalInput").ap()
    out_h = nc.dram_tensor("out_slab", [TLOC, D], F32,
                           kind="ExternalOutput").ap()

    agin = nc.dram_tensor("agin", [g.SHARD], BF16).ap()
    agout = nc.dram_tensor("agout", [4 * g.SHARD], BF16).ap()
    qscr = nc.dram_tensor("q_scr", [D, TLOC], BF16).ap()
    oscr = nc.dram_tensor("o_scr", [NQS, 128, D], BF16).ap()
    xnscr = nc.dram_tensor("xn_scr", [D, TLOC], BF16).ap()
    rowscr = nc.dram_tensor("row_scr", [64], F32).ap()
    agin_st = nc.dram_tensor("agin_st", [64], BF16).ap()
    agout_st = nc.dram_tensor("agout_st", [256], BF16).ap()
    agout_k = agout.rearrange("(r c) -> r c", c=512)
    agout_v = agout.rearrange("(r c) -> r c", c=260)

    groups = [[0, 1, 2, 3], [4, 5, 6, 7]]
    sch = Sched(nc)

    with ExitStack() as stx:
        def sb(name, shape, dt):
            return stx.enter_context(nc.sbuf_tensor(name, shape, dt))

        x_in = [sb(f"x_in{d}", [128, TLOC], F32) for d in range(DCH)]
        xbf = [sb(f"xbf{d}", [128, HSF], BF16) for d in range(DCH)]
        params = sb("params_sb", [128, PCOLS], F32)
        brow = sb("brow_sb", [65, D], BF16)
        onesrow = sb("onesrow", [1, TC], BF16)
        ones65 = sb("ones65", [65, TC], BF16)
        idxtab = sb("idxtab_sb", [128, g.NIDX], U32)
        flagst = sb("flagst", [1, 512], BF16)
        mrowst = sb("mrowst", [1, 512], BF16)
        biasall = sb("biasall", [128, g.NBC], F32)
        rows = sb("rows_sb", [1, 256], F32)
        mask512 = [sb(f"mask_512_{i}", [128, 512], BF16) for i in range(4)]
        masks = {}
        mask_reg = {}
        for w in (128, 256, 512):
            for i in range(w // 128):
                j = i + (512 - w) // 128
                masks[(w, i)] = mask512[j][:, 0:w]
                mask_reg[(w, i)] = f"mask512_{j}"
        identb = sb("identb", [128, 128], BF16)
        identf = sb("identf", [128, 128], F32)
        blkind = [sb(f"blkind{c}", [128, 16], BF16) for c in range(DCH)]
        piota = sb("piota", [128, 1], F32)
        allones = sb("allones", [128, 128], BF16)
        sqt = [sb(f"sqt{i}", [128, TC], BF16) for i in range(2)]
        evac_t = sb("evac_t", [128, 4 * TC], BF16)
        ebuf_t = sb("ebuf_t", [128, 6 * 512], BF16)
        mu_bc = [sb(f"mu_bc{t}", [128, TC], BF16) for t in range(2)]
        r_bc = [sb(f"r_bc{t}", [128, TC], BF16) for t in range(2)]
        stat_ft = sb("stat_ft", [128, 3 * TC], F32)
        stat_f = [stat_ft[:, TC * i:TC * (i + 1)] for i in range(3)]
        evac = [evac_t[:, TC * i:TC * (i + 1)] for i in range(4)]
        vstage = sb("vstage", [128, 520], BF16)
        KBC = max(NBK * 128, 8 * 1024, 4 * TLOC)
        VBC = max(2 * NBK * 65, 8 * HSF, 8 * 1024)
        kbuf = [sb(f"kbuf{i}", [128, KBC], BF16) for i in range(2)]
        vbuf = [sb(f"vbuf{i}", [128, VBC], BF16) for i in range(2)]
        qpair = [sb(f"qpair{i}", [128, max(TLOC, 2 * D)], BF16)
                 for i in range(2)]
        ebuf = [ebuf_t[:, 512 * i:512 * (i + 1)] for i in range(6)]
        rcpb = [sb(f"rcp{i}", [128, 1], F32) for i in range(4)]
        otm = [sb(f"otm{i}", [128, 512], BF16) for i in range(2)]
        maxsq = [sb(f"maxsq{i}", [16, 1], F32) for i in range(2)]
        red16 = sb("red16", [16, 1], F32)
        col1 = [sb(f"col1_{i}", [128, 1], F32) for i in range(6)]
        wtile = [sb(f"wtile{i}", [128, 1024], BF16) for i in range(4)]
        if g.NBC >= 1024:
            w2half = [biasall[:, 0:1024].bitcast(BF16),
                      stat_ft[:, 0:1024].bitcast(BF16)]
        else:
            w2h_real = sb("w2h_real", [128, 2048], BF16)
            w2half = [w2h_real[:, :], stat_ft[:, 0:1024].bitcast(BF16)]
        wdtile = stat_ft[:, :].bitcast(BF16)
        wtile2v = [wdtile[:, 0:1024], wdtile[:, 1024:2048]]
        xtm = evac_t[:, 0:2 * D].bitcast(F32)
        xsq = ebuf_t[:, 0:2 * D].bitcast(F32)
        if g.NBC * 4 >= 4 * D:
            xtmf = biasall[:, 0:D].bitcast(F32) if False else biasall[:, 0:D]
        else:
            xtmf = sb("xtmf", [128, D], F32)[:, :]
        outtm = [qpair[i][:, 0:2 * D].bitcast(F32) for i in range(2)]
        PS = [stx.enter_context(nc.psum_tensor(f"ps{i}", [128, 512], F32))
              for i in range(7)]
        TRB = stx.enter_context(nc.psum_tensor("trb", [128, 1024], BF16))
        S0, S1, PA, PB, PC, PD, PM = range(7)
        psr = [f"ps{i}" for i in range(7)] + ["trb"]

        def DMA(engine, out, in_, reads, writes, tag="", slow=False):
            def f(eng):
                return eng.dma_start(out=out, in_=in_,
                                     allow_slow_non_contiguous=slow)
            return sch.add(engine, f, reads=reads, writes=writes, dma=True,
                           tag=tag)

        def MM(bank, psum_ap, lhsT, rhs, start, stop, reads, tag=""):
            def f(eng):
                return eng.matmul(psum_ap, lhsT, rhs, start=start, stop=stop)
            return sch.add("tensor", f, reads=reads, writes=[psr[bank]],
                           tag=tag)

        def TRANS(out_ap, in_ap, ident, reads, writes, tag=""):
            def f(eng):
                return eng.transpose(out_ap, in_ap, ident)
            return sch.add("tensor", f, reads=reads, writes=writes, tag=tag)

        def ACT(out, in_, func, reads, writes, bias=None, scale=None, tag=""):
            kw = {}
            if bias is not None:
                kw["bias"] = bias
            if scale is not None:
                kw["scale"] = scale

            def f(eng):
                return eng.activation(out, in_, func, **kw)
            return sch.add("scalar", f, reads=reads, writes=writes, tag=tag)

        def VOP(name, args, reads, writes, tag="", engine="vector"):
            def f(eng):
                return getattr(eng, name)(*args)
            return sch.add(engine, f, reads=reads, writes=writes, tag=tag)

        def GATHER(out, in_, idxcol, reads, writes, tag=""):
            def f(eng):
                return eng.indirect_dma_start(
                    out=out, out_offset=None, in_=in_,
                    in_offset=bass.IndirectOffsetOnAxis(idxcol, 0))
            return sch.add("gpsimd", f, reads=reads, writes=writes, dma=True,
                           tag=tag)

        def COLL(kind, op, ins, outs, reads, writes, tag="cc"):
            def f(eng):
                return eng.collective_compute(kind, op, replica_groups=groups,
                                              ins=ins, outs=outs)
            return sch.add("gpsimd", f, reads=reads, writes=writes, cc=True,
                           tag=tag)

        def kb_regions(i):
            return [f"kb{i}g{m}" for m in range((KBC // 128 + 3) // 4)]

        def vb_regions(i):
            return [f"vb{i}a{a}g{m}" for a in range(2)
                    for m in range((NBK + 3) // 4)]

        NFC1 = F // 128

        def h_slot(fc):
            per_kb = KBC // HSF
            per_vb = VBC // HSF
            if fc < per_kb:
                return kbuf[0], fc * HSF
            fc -= per_kb
            if fc < per_kb:
                return kbuf[1], fc * HSF
            fc -= per_kb
            if fc < per_vb:
                return vbuf[0], fc * HSF
            fc -= per_vb
            return vbuf[1], fc * HSF

        DMA("sync", params[:, :], params_h, ["params_h"], ["params"], "params")
        DMA("sync", idxtab[:, :], idx_h, ["idx_h"], ["idxtab"], "idxtab")
        off = 0
        for (w, i) in mc:
            j = i + (512 - w) // 128
            DMA("gpsimd", mask512[j][:, :], cpack_h[:, off:off + w],
                ["cpack_h"], [f"mask512_{j}"], f"mask512_{j}")
            off += w
        DMA("gpsimd", identb[:, :], cpack_h[:, off:off + 128],
            ["cpack_h"], ["identb"], "identb")
        DMA("sync", identf[:, :], cpack_h[:, off:off + 128],
            ["cpack_h"], ["identf"], "identf")
        off += 128
        for c in range(DCH):
            DMA("gpsimd", blkind[c][:, :], cpack_h[:, off:off + 16],
                ["cpack_h"], [f"blkind{c}"], f"blkind{c}")
            off += 16
        DMA("sync", piota[:, :], cpack_h[:, off:off + 1],
            ["cpack_h"], ["piota"], "piota", slow=True)
        VOP("memset", (onesrow[:, :], 1.0), [], ["onesrow"], "ones")
        VOP("memset", (allones[:, :], 1.0), [], ["allones"], "allones")
        VOP("memset", (ones65[:, :], 1.0), [], ["ones65"], "ones65")
        VOP("memset", (vstage[:, :], 1.0), [], ["vstage"], "vst_init")
        zview = kbuf[0][:, 0:4096]
        VOP("memset", (zview, 0.0), [], ["zbuf"], "zbuf")
        DMA("sync", bass.AP(agin_st.tensor, 0, [[1, 64]]),
            zview[0:1, 0:64], ["zbuf"], ["agin_s"], "zagst")
        ZCH = 128 * 4096
        nz = (g.SHARD + ZCH - 1) // ZCH
        for zi in range(nz):
            n = min(ZCH, g.SHARD - zi * ZCH)
            zrows = n // 4096
            rem = n - zrows * 4096
            if zrows:
                DMA("sync",
                    bass.AP(agin.tensor, zi * ZCH, [[4096, zrows], [1, 4096]]),
                    zview[0:zrows, :], ["zbuf"],
                    [f"aginb{b}" for b in range(8)] + ["agin_s"], f"zag{zi}")
            if rem:
                DMA("sync",
                    bass.AP(agin.tensor, zi * ZCH + zrows * 4096, [[1, rem]]),
                    zview[0:1, 0:rem], ["zbuf"],
                    [f"aginb{b}" for b in range(8)] + ["agin_s"], f"zagr{zi}")

        recD = 1.0 / D
        for q in range(NQS):
            DMA("sync", xtm, x_h[128 * q:128 * q + 128, :],
                ["x_h"], ["xtm"], f"xld{q}")
            ACT(xsq, xtm, AF.Square, ["xtm"], ["xsq"],
                tag=f"xsq{q}")
            VOP("reduce_sum", (col1[0][:, :], xtm, AX.X),
                ["xtm"], ["col0"], f"xsum{q}")
            VOP("reduce_sum", (col1[1][:, :], xsq, AX.X),
                ["xsq"], ["col1"], f"xss{q}")
            VOP("tensor_scalar", (col1[0][:, :], col1[0][:, :], recD, None,
                                  ALU.mult), ["col0"], ["col0"], f"mu{q}")
            VOP("tensor_scalar", (col1[1][:, :], col1[1][:, :], recD, None,
                                  ALU.mult), ["col1"], ["col1"], f"ms{q}")
            VOP("tensor_tensor", (col1[2][:, :], col1[0][:, :], col1[0][:, :],
                                  ALU.mult), ["col0"], ["col2"], f"mu2{q}")
            VOP("tensor_tensor", (col1[1][:, :], col1[1][:, :], col1[2][:, :],
                                  ALU.subtract), ["col1", "col2"], ["col1"],
                f"var{q}")
            VOP("tensor_scalar", (col1[1][:, :], col1[1][:, :], LN_EPS,
                                  None, ALU.add), ["col1"], ["col1"],
                f"veps{q}")
            ACT(col1[2][:, :], col1[1][:, :], AF.Sqrt, ["col1"], ["col2"],
                tag=f"sd{q}")
            VOP("reciprocal", (col1[3][:, :], col1[2][:, :]),
                ["col2"], ["col3"], f"rstd{q}")
            VOP("tensor_scalar", (xtm, xtm, col1[0][:, 0:1],
                                  col1[3][:, 0:1], ALU.subtract, ALU.mult),
                ["xtm", "col0", "col3"], ["xtm"], f"xn{q}")
            for dd in range(DCH):
                bank = PA + (dd % 2)
                TRANS(PS[bank][:, 0:128], xtm[:, 128 * dd:128 * dd + 128],
                      identf[:, :], ["xtm", "identf"], [psr[bank]],
                      f"xtr{q}_{dd}")
                VOP("tensor_copy",
                    (x_in[dd][:, 128 * q:128 * q + 128], PS[bank][:, 0:128]),
                    [psr[bank]], [f"x_in{dd}"], f"xev{q}_{dd}")
        for dd in range(DCH):
            DMA("gpsimd", xnscr[128 * dd:128 * dd + 128, :], x_in[dd][:, :],
                [f"x_in{dd}"], ["xnscr"], f"xnscr{dd}")

        def emit_ln(tagp, th, wcol=None, bcol=None):
            sch.barrier(["w2t0", "w2t1", "wu0", "wu1"],
                        ["stf0", "stf1", "stf2"])
            tof = HSF * th
            for dd in range(DCH):
                VOP("tensor_copy", (xbf[dd][:, :],
                                    x_in[dd][:, tof:tof + HSF]),
                    [f"x_in{dd}"], [f"xbf{dd}"], f"{tagp}cast{dd}")
            for t in range(NSUB):
                tc = slice(TC * t, TC * (t + 1))
                tb = t % 2
                for dd in range(DCH):
                    sq = sqt[dd % 2]
                    ACT(sq[:, :], xbf[dd][:, tc], AF.Square,
                        [f"xbf{dd}"], [f"sqt{dd % 2}"], tag=f"{tagp}sq{t}_{dd}")
                    MM(PM, PS[PM][:, :], allones[:, :], xbf[dd][:, tc],
                       dd == 0, dd == DCH - 1, ["allones", f"xbf{dd}"],
                       f"{tagp}sum{t}_{dd}")
                    MM(PA, PS[PA][:, :], allones[:, :], sq[:, :],
                       dd == 0, dd == DCH - 1, ["allones", f"sqt{dd % 2}"],
                       f"{tagp}ss{t}_{dd}")
                VOP("tensor_scalar", (mu_bc[tb][:, :], PS[PM][:, :], recD,
                                      None, ALU.mult), [psr[PM]],
                    [f"mu_bc{tb}"], f"{tagp}mu{t}")
                VOP("tensor_scalar", (stat_f[0], PS[PM][:, :], recD,
                                      None, ALU.mult), [psr[PM]], ["stf0"],
                    f"{tagp}muf{t}")
                VOP("tensor_scalar", (stat_f[1], PS[PA][:, :], recD,
                                      None, ALU.mult), [psr[PA]], ["stf1"],
                    f"{tagp}ssf{t}")
                VOP("tensor_tensor", (stat_f[2], stat_f[0],
                                      stat_f[0], ALU.mult),
                    ["stf0"], ["stf2"], f"{tagp}mu2{t}")
                VOP("tensor_tensor", (stat_f[1], stat_f[1],
                                      stat_f[2], ALU.subtract),
                    ["stf1", "stf2"], ["stf1"], f"{tagp}var{t}")
                VOP("tensor_scalar", (stat_f[1], stat_f[1],
                                      LN_EPS, None, ALU.add),
                    ["stf1"], ["stf1"], f"{tagp}veps{t}")
                ACT(stat_f[2], stat_f[1], AF.Sqrt,
                    ["stf1"], ["stf2"], tag=f"{tagp}sd{t}")
                VOP("reciprocal", (stat_f[0], stat_f[2]),
                    ["stf2"], ["stf0"], f"{tagp}rr{t}")
                VOP("tensor_copy", (r_bc[tb][:, :], stat_f[0]),
                    ["stf0"], [f"r_bc{tb}"], f"{tagp}rbc{t}")
            for dd in range(DCH):
                for t in range(NSUB):
                    tc = slice(TC * t, TC * (t + 1))
                    tb = t % 2
                    VOP("tensor_tensor", (xbf[dd][:, tc], xbf[dd][:, tc],
                                          mu_bc[tb][:, :], ALU.subtract),
                        [f"xbf{dd}", f"mu_bc{tb}"], [f"xbf{dd}"],
                        f"{tagp}ap1_{dd}_{t}")
                    VOP("tensor_tensor", (xbf[dd][:, tc], xbf[dd][:, tc],
                                          r_bc[tb][:, :], ALU.mult),
                        [f"xbf{dd}", f"r_bc{tb}"], [f"xbf{dd}"],
                        f"{tagp}ap2_{dd}_{t}")
                    if wcol is not None:
                        VOP("tensor_scalar",
                            (xbf[dd][:, tc], xbf[dd][:, tc],
                             params[:, wcol + dd:wcol + dd + 1],
                             params[:, bcol + dd:bcol + dd + 1],
                             ALU.mult, ALU.add),
                            [f"xbf{dd}", "params"], [f"xbf{dd}"],
                            f"{tagp}ap3_{dd}_{t}")


        def emit_qkv_weights(ly):
            sch.barrier(["w2t0", "w2t1", "wu0", "wu1"],
                        ["stf0", "stf1", "stf2", "biasall"])
            sch.barrier(kb_regions(0) + kb_regions(1) + vb_regions(0)
                        + vb_regions(1) + [f"hkb{i}" for i in range(NFC1)]
                        + ["zbuf", "xtm", "xsq", "w2t0", "w2t1", "wu0",
                           "wu1"]
                        + [f"ofm{d}" for d in range(DCH)],
                        [f"qkw{fc}" for fc in range(16)]
                        + [f"vw{dd}" for dd in range(DCH)])
            for fc in range(16):
                dst = kbuf[fc // 8][:, 1024 * (fc % 8):1024 * (fc % 8 + 1)]
                DMA("gpsimd",
                    dst.rearrange("p (c f) -> p c f", f=128),
                    wqkv_h[ly, :, 128 * fc:128 * fc + 128]
                    .rearrange("(c p) f -> p c f", p=128),
                    ["wqkv_h"], [f"qkw{fc}"], f"qkwld{ly}_{fc}")
            for dd in range(DCH):
                dst = vbuf[0][:, 1024 * dd:1024 * (dd + 1)]
                DMA("gpsimd", dst,
                    wqkv_h[ly, 128 * dd:128 * dd + 128, 2048:3072],
                    ["wqkv_h"], [f"vw{dd}"], f"vwld{ly}_{dd}")

        def emit_qkv_half(ly, th):
            tof = HSF * th
            qkbias0 = ly * PCL + 32
            if th == 0:
                for which in (0, 1):
                    VOP("memset", (maxsq[which][:, :], 0.0), [],
                        [f"maxsq{which}"], f"nmz{ly}_{which}")
            for fc in range(16):
                wt = kbuf[fc // 8][:, 1024 * (fc % 8):1024 * (fc % 8 + 1)]
                for t in range(NSUB):
                    bank = PB + ((fc if NSUB == 1 else t) % 2)
                    tc = slice(TC * t, TC * (t + 1))
                    gtc = slice(tof + TC * t, tof + TC * t + TC)
                    for dd in range(DCH):
                        MM(bank, PS[bank][:, :],
                           wt[:, 128 * dd:128 * dd + 128], xbf[dd][:, tc],
                           dd == 0, dd == DCH - 1,
                           [f"qkw{fc}", f"xbf{dd}"], f"qk{ly}_{th}_{fc}_{t}_{dd}")
                    ev = evac[t % 4]
                    ACT(ev, PS[bank][:, :], AF.Identity,
                        [psr[bank], "params"], [f"evac{t % 4}"],
                        bias=params[:, qkbias0 + fc:qkbias0 + fc + 1],
                        scale=1.0, tag=f"qkev{ly}_{th}_{fc}_{t}")
                    sq = sqt[(fc + t) % 2]
                    ACT(sq[:, :], ev, AF.Square, [f"evac{t % 4}"],
                        [f"sqt{(fc + t) % 2}"], tag=f"qnsq{ly}_{th}_{fc}_{t}")
                    nbank = PM if t == 0 else PD
                    MM(nbank, PS[nbank][0:16, :], blkind[fc % 8][:, :],
                       sq[:, :], fc % 8 == 0, fc % 8 == 7,
                       [f"blkind{fc % 8}", f"sqt{(fc + t) % 2}"],
                       f"qnmm{ly}_{th}_{fc}_{t}")
                    if fc % 8 == 7:
                        which = 0 if fc < 8 else 1
                        VOP("reduce_max", (red16[:, :], PS[nbank][0:16, :],
                                           AX.X), [psr[nbank]], ["red16"],
                            f"qnred{ly}_{th}_{which}_{t}")
                        VOP("tensor_tensor",
                            (maxsq[which][:, :], maxsq[which][:, :],
                             red16[:, :], ALU.max),
                            [f"maxsq{which}", "red16"], [f"maxsq{which}"],
                            f"qnmax{ly}_{th}_{which}_{t}")
                    if fc < 8:
                        DMA("sync", qscr[128 * fc:128 * fc + 128, gtc],
                            ev, [f"evac{t % 4}"], ["qscr"],
                            f"qst{ly}_{th}_{fc}_{t}")
                    else:
                        blk = fc - 8
                        ko = blk * g.PBLK + 2 * g.VSZ
                        dstap = bass.AP(agin.tensor, ko + tof + TC * t,
                                        [[TLOC, 128], [1, TC]])
                        DMA("sync", dstap, ev,
                            [f"evac{t % 4}"], [f"aginb{blk}"],
                            f"kst{ly}_{th}_{fc}_{t}")
            vb0 = ly * D
            for tt in range(tof // 128, (tof + HSF) // 128):
                ltt = tt - tof // 128
                for dd in range(DCH):
                    lhs = xbf[dd][:, 128 * ltt:128 * ltt + 128]
                    for half, bank in ((0, PA), (1, PC)):
                        MM(bank, PS[bank][:, :], lhs,
                           vbuf[0][:, 1024 * dd + 512 * half:
                                   1024 * dd + 512 * half + 512],
                           dd == 0, False,
                           [f"vw{dd}", f"xbf{dd}"],
                           f"v{ly}_{tt}_{dd}_{half}")
                MM(PA, PS[PA][:, :], onesrow[0:1, 0:128],
                   brow[0:1, 0:512], False, True,
                   ["onesrow", "brow0"], f"vb{ly}_{tt}_a")
                MM(PC, PS[PC][:, :], onesrow[0:1, 0:128],
                   brow[0:1, 512:1024], False, True,
                   ["onesrow", "brow0"], f"vb{ly}_{tt}_b")
                for half, bank in ((0, PA), (1, PC)):
                    VOP("tensor_copy",
                        (vstage[:, :].rearrange("p (h c) -> p h c", c=65)
                         [:, :, 0:64],
                         PS[bank][:, :].rearrange("p (h c) -> p h c", c=64)),
                        [psr[bank]], ["vstage"], f"vev{ly}_{tt}_{half}")
                    s_, j_ = tt // 4, tt % 4
                    for a_ in range(2):
                        dstap = bass.AP(
                            agin.tensor,
                            4 * half * g.PBLK + a_ * g.VSZ
                            + (512 * s_ + j_) * 65,
                            [[4 * 65, 128], [g.PBLK, 4], [1, 65]])
                        srcap = vstage[:, :].rearrange(
                            "p (b a c) -> p b a c", a=2, c=65)[:, :, a_, :]
                        DMA("sync", dstap, srcap,
                            ["vstage"],
                            [f"aginb{4 * half + bb}" for bb in range(4)],
                            f"vst{ly}_{tt}_{half}_{a_}")

        def emit_norms_ag(ly):
            DMA("gpsimd", bass.AP(agin_st.tensor, 0, [[1, 16]]),
                maxsq[1][:, :], ["maxsq1"], ["agin_s"], f"nst{ly}")
            COLL("AllGather", ALU.bypass,
                 [agin_st.rearrange("(a b) -> a b", b=64).opt()],
                 [agout_st.rearrange("(a b) -> a b", b=64).opt()],
                 ["agin_s"], ["agout_s"], f"agst{ly}")
            for pi, (ha, hb) in enumerate(g.pairs):
                blk = ha // 2
                ain = bass.AP(agin.tensor, blk * g.PBLK,
                              [[512, g.PBLK // 512], [1, 512]])
                aout = bass.AP(agout.tensor, blk * 4 * g.PBLK,
                               [[512, 4 * g.PBLK // 512], [1, 512]])
                COLL("AllGather", ALU.bypass, [ain.opt()], [aout.opt()],
                     [f"aginb{blk}"], [f"agoutb{blk}"], f"ag{ly}_{blk}")
            DMA("gpsimd", rowscr[0:16], maxsq[0][:, :], ["maxsq0"],
                ["rowscr"], f"nbq{ly}")
            DMA("sync", rows[0:1, 16:32], rowscr[0:16], ["rowscr"],
                ["rows"], f"nbq2{ly}")
            statrow = bass.AP(agout_st.tensor, 0,
                              [[1, 1], [1, 16], [64, 4]])

            def f_st(eng):
                return eng.dma_start(out=rows[0:1, 64:128].rearrange(
                    "a (h r) -> a h r", r=4), in_=statrow,
                    allow_slow_non_contiguous=True)
            sch.add("gpsimd", f_st, reads=["agout_s"], writes=["rows"],
                    dma=True, tag=f"nks{ly}")
            VOP("reduce_max", (rows[0:1, 0:16],
                               rows[0:1, 64:128].rearrange(
                                   "a (h r) -> a h r", r=4), AX.X),
                ["rows"], ["rows"], f"nkr{ly}")
            ACT(rows[0:1, 32:64], rows[0:1, 0:32], AF.Sqrt, ["rows"],
                ["rows"], tag=f"nsqrt{ly}")
            VOP("tensor_tensor", (rows[0:1, 48:64], rows[0:1, 32:48],
                                  rows[0:1, 48:64], ALU.mult),
                ["rows"], ["rows"], f"nmul{ly}")
            VOP("tensor_scalar", (rows[0:1, 48:64], rows[0:1, 48:64],
                                  0.125, M_MARGIN, ALU.mult, ALU.add),
                ["rows"], ["rows"], f"nmbar{ly}")
            hseg = {}
            for (hh, s, u, i), c in g.bias_col.items():
                a, b = hseg.get(hh, (c, c))
                hseg[hh] = (min(a, c), max(b, c))
            nchunk = (g.NBC + 511) // 512
            for cch in range(nchunk):
                c0 = 512 * cch
                cw = min(512, g.NBC - c0)
                DMA("gpsimd", flagst[0:1, 0:cw], flag_h[0:1, c0:c0 + cw],
                    ["flag_h"], ["flagst"], f"fstg{ly}_{cch}")
                for h in range(H):
                    a, b = hseg[h]
                    b += 1
                    lo, hi = max(a, c0), min(b, c0 + cw)
                    if lo >= hi:
                        continue
                    VOP("tensor_scalar",
                        (mrowst[0:1, lo - c0:hi - c0],
                         flagst[0:1, lo - c0:hi - c0],
                         KILL, rows[0:1, 48 + h:49 + h],
                         ALU.mult, ALU.add),
                        ["flagst", "rows"], ["mrowst"], f"mrow{ly}_{cch}_{h}")
                DMA("sync", stat_f[2][:, 0:cw], rampall_h[:, c0:c0 + cw],
                    ["rampall_h"], ["stf2"], f"rstg{ly}_{cch}")
                MM(PM, PS[PM][:, 0:cw], onesrow[0:1, 0:128],
                   mrowst[0:1, 0:cw], True, True,
                   ["onesrow", "mrowst"], f"bb{ly}_{cch}")
                VOP("tensor_tensor", (biasall[:, c0:c0 + cw],
                                      stat_f[2][:, 0:cw],
                                      PS[PM][:, 0:cw], ALU.subtract),
                    ["stf2", psr[PM]], ["biasall"], f"bs{ly}_{cch}")

        def emit_attention(ly):
            VW = NBK * 65
            for pi, (ha, hb) in enumerate(g.pairs):
                bi = pi % 2
                if pi < 2:
                    old = [f"qkw{fc}" for fc in range(16)] \
                        + [f"vw{dd}" for dd in range(DCH)]
                    sch.barrier(old, kb_regions(bi) + vb_regions(bi)
                                + [f"qp{bi}"])
                DMA("sync", qpair[bi][0:64, 0:TLOC],
                    qscr[64 * ha:64 * ha + 64, :], ["qscr"], [f"qp{bi}"],
                    f"qld{ly}_{pi}a")
                DMA("sync", qpair[bi][64:128, 0:TLOC],
                    qscr[64 * hb:64 * hb + 64, :], ["qscr"], [f"qp{bi}"],
                    f"qld{ly}_{pi}b")
                for a, h in ((0, ha), (1, hb)):
                    for (dt4, pidx) in g.kload[h]:
                        m = dt4 // 4
                        GATHER(kbuf[bi][64 * a:64 * a + 64,
                                        128 * dt4:128 * dt4 + 512],
                               agout_k, idxtab[0:64, pidx:pidx + 1],
                               [f"agoutb{h // 2}", "idxtab"],
                               [f"kb{bi}g{m}"],
                               f"kg{ly}_{pi}_{a}_{m}")
                    for (dt4, pidx) in g.vload[h]:
                        m = dt4 // 4
                        GATHER(vbuf[bi][:, VW * a + 65 * dt4:
                                        VW * a + 65 * dt4 + 260],
                               agout_v, idxtab[:, pidx:pidx + 1],
                               [f"agoutb{h // 2}", "idxtab"],
                               [f"vb{bi}a{a}g{m}"],
                               f"vg{ly}_{pi}_{a}_{m}")
                erot = [0]
                for sch_s in range(NS):
                    for a, h in ((0, ha), (1, hb)):
                        W = g.Wq[h]
                        beta = W // 128
                        U = g.CH // W
                        for u in range(U):
                            n = g.nmat[(h, sch_s, u)]
                            qc = 512 * sch_s + W * u
                            rhs = qpair[bi][64 * a:64 * a + 64, qc:qc + W]
                            obanks = [PA, PB, PC, PD][:beta]
                            LAG = 3
                            epipe = []
                            for i in range(n + LAG):
                                if i < n:
                                    B = g.bslot(h, sch_s, u, i)
                                    col = g.bias_col[(h, sch_s, u, i)]
                                    sbank = [S0, S1, PM][i % 3]
                                    mk = f"kb{bi}g{B // 4}"
                                    MM(sbank, PS[sbank][:, 0:W],
                                       kbuf[bi][64 * a:64 * a + 64,
                                                128 * B:128 * B + 128],
                                       rhs, True, True,
                                       [mk, f"qp{bi}"],
                                       f"S{ly}_{pi}_{a}_{sch_s}_{u}_{i}")
                                    eb = ebuf[erot[0] % 6]
                                    ebr = f"eb{erot[0] % 6}"
                                    erot[0] += 1
                                    ACT(eb[:, 0:W], PS[sbank][:, 0:W],
                                        AF.Exp,
                                        [psr[sbank], "biasall"], [ebr],
                                        bias=biasall[:, col:col + 1],
                                        scale=0.125,
                                        tag=f"E{ly}_{pi}_{a}_{sch_s}_{u}_{i}")
                                    if i < beta:
                                        VOP("tensor_tensor",
                                            (eb[:, 0:W], eb[:, 0:W],
                                             masks[(W, i)], ALU.mult),
                                            [ebr, mask_reg[(W, i)]], [ebr],
                                            f"M{ly}_{pi}_{a}_{sch_s}_{u}_{i}")
                                    epipe.append((i, eb, ebr))
                                if i >= LAG:
                                    j, ejb, ejr = epipe[i - LAG]
                                    Bj = g.bslot(h, sch_s, u, j)
                                    mv = f"vb{bi}a{a}g{Bj // 4}"
                                    for qs in range(beta):
                                        MM(obanks[qs],
                                           PS[obanks[qs]][:, 0:65],
                                           ejb[:, 128 * qs:128 * qs + 128],
                                           vbuf[bi][:, VW * a + 65 * Bj:
                                                    VW * a + 65 * Bj + 65],
                                           j == 0, j == n - 1,
                                           [ejr, mv],
                                           f"PV{ly}_{pi}_{a}_{sch_s}_{u}_{j}_{qs}")
                            for qs in range(beta):
                                k_in_s = (W * u + 128 * qs) // 128
                                rc = rcpb[qs]
                                VOP("reciprocal",
                                    (rc[:, :], PS[obanks[qs]][:, 64:65]),
                                    [psr[obanks[qs]]], [f"rc{qs}"],
                                    f"R{ly}_{pi}_{a}_{sch_s}_{u}_{qs}")
                                VOP("tensor_scalar",
                                    (otm[sch_s % 2][:, 128 * k_in_s + 64 * a:
                                                    128 * k_in_s + 64 * a + 64],
                                     PS[obanks[qs]][:, 0:64], rc[:, 0:1],
                                     None, ALU.mult),
                                    [psr[obanks[qs]], f"rc{qs}"],
                                    [f"otm{sch_s % 2}"],
                                    f"ON{ly}_{pi}_{a}_{sch_s}_{u}_{qs}")
                    for a, h in ((0, ha), (1, hb)):
                        dstap = bass.AP(
                            oscr.tensor, (4 * sch_s) * 128 * D + 64 * h,
                            [[D, 128], [128 * D, 4], [1, 64]])
                        DMA("sync", dstap,
                            otm[sch_s % 2][:, :].rearrange(
                                "p (k c) -> p k c", c=128)
                            [:, :, 64 * a:64 * a + 64],
                            [f"otm{sch_s % 2}"], [f"oscr{sch_s}_{h}"],
                            f"ost{ly}_{pi}_{sch_s}_{a}")

        def emit_outproj(ly):
            sch.barrier(kb_regions(0) + kb_regions(1),
                        [f"ofm{d}" for d in range(DCH)])
            for q in range(NQS):
                rb = qpair[q % 2]
                if q < 2:
                    sch.barrier([f"qp{q % 2}"], [f"oin{q % 2}"])
                DMA("sync", rb[:, 0:D], oscr[q, :, :],
                    [f"oscr{q // 4}_{hh}" for hh in range(H)],
                    [f"oin{q % 2}"], f"old{ly}_{q}")
                for dd in range(DCH):
                    TRANS(TRB[:, 128 * dd:128 * dd + 128],
                          rb[:, 128 * dd:128 * dd + 128], identb[:, :],
                          [f"oin{q % 2}", "identb"], ["trb"],
                          f"otr{ly}_{q}_{dd}")
                    ofm = kbuf[dd // 4][:, TLOC * (dd % 4) + 128 * q:
                                       TLOC * (dd % 4) + 128 * q + 128]
                    VOP("tensor_copy",
                        (ofm, TRB[:, 128 * dd:128 * dd + 128]),
                        ["trb"], [f"ofm{dd}"], f"oev{ly}_{q}_{dd}")
            ob0 = ly * D
            for f in range(DCH):
                wt = wtile[f % 4]
                DMA("gpsimd",
                    wt[:, :].rearrange("p (c f) -> p c f", f=128),
                    wo_h[ly, :, 128 * f:128 * f + 128]
                    .rearrange("(c p) f -> p c f", p=128),
                    ["wo_h"], [f"wt{f % 4}"], f"wold{ly}_{f}")
                for t in range(NT):
                    bank = PA + (t % 2)
                    tc = slice(TC * t, TC * (t + 1))
                    for dd in range(DCH):
                        ofm = kbuf[dd // 4][:, TLOC * (dd % 4) + TC * t:
                                            TLOC * (dd % 4) + TC * t + TC]
                        MM(bank, PS[bank][:, :],
                           wt[:, 128 * dd:128 * dd + 128], ofm,
                           dd == 0, False,
                           [f"wt{f % 4}", f"ofm{dd}"], f"op{ly}_{f}_{t}_{dd}")
                    MM(bank, PS[bank][:, :], brow[32:33, 128 * f:
                                                  128 * f + 128],
                       ones65[32:33, :], False, True,
                       ["brow1", "ones65"], f"opb{ly}_{f}_{t}")
                    VOP("tensor_tensor",
                        (x_in[f][:, tc], x_in[f][:, tc], PS[bank][:, :],
                         ALU.add), [f"x_in{f}", psr[bank]], [f"x_in{f}"],
                        f"ores{ly}_{f}_{t}")

        def emit_ffn_half(ly, th):
            tof = HSF * th
            fb0 = ly * PCL + 48
            if th == 0:
                sch.barrier([f"ofm{d}" for d in range(DCH)]
                            + kb_regions(0) + kb_regions(1)
                            + vb_regions(0) + vb_regions(1),
                            [f"hkb{i}" for i in range(NFC1)]
                            + ["w2t0", "w2t1"])
            for fc in range(NFC1):
                wt = wtile[fc % 4]
                DMA("gpsimd",
                    wt[:, :].rearrange("p (c f) -> p c f", f=128),
                    w1_h[ly, :, 128 * fc:128 * fc + 128]
                    .rearrange("(c p) f -> p c f", p=128),
                    ["w1_h"], [f"wt{fc % 4}"], f"w1ld{ly}_{th}_{fc}")
                hb, hcol = h_slot(fc)
                for sub in range(NSUB):
                    bank = PA + ((fc if NSUB == 1 else sub) % 2)
                    tc = slice(TC * sub, TC * sub + TC)
                    for dd in range(DCH):
                        MM(bank, PS[bank][:, :],
                           wt[:, 128 * dd:128 * dd + 128],
                           xbf[dd][:, tc], dd == 0, dd == DCH - 1,
                           [f"wt{fc % 4}", f"xbf{dd}"],
                           f"f1_{ly}_{th}_{fc}_{sub}_{dd}")
                    ACT(hb[:, hcol + TC * sub:hcol + TC * sub + TC],
                        PS[bank][:, :], AF.Gelu,
                        [psr[bank], "params"], [f"hkb{fc}"],
                        bias=params[:, fb0 + fc:fb0 + fc + 1], scale=1.0,
                        tag=f"f1e{ly}_{th}_{fc}_{sub}")
            sch.barrier(["stf0", "stf1", "stf2", "biasall"],
                        ["w2t0", "w2t1"])
            for f in range(DCH):
                for hh in range(2):
                    w2t = w2half[hh]
                    DMA("gpsimd",
                        w2t.rearrange("p (c f) -> p c f", f=128),
                        w2_h[ly, 2048 * hh:2048 * (hh + 1),
                             128 * f:128 * f + 128]
                        .rearrange("(c p) f -> p c f", p=128),
                        ["w2_h"], [f"w2t{hh}"], f"w2ld{ly}_{th}_{f}_{hh}")
                for sub in range(NSUB):
                    bank = PC + ((f if NSUB == 1 else sub) % 2)
                    tc = slice(TC * sub, TC * sub + TC)
                    for hk in range(NFC1):
                        hb, hcol = h_slot(hk)
                        w2t = w2half[hk // 16]
                        MM(bank, PS[bank][:, :],
                           w2t[:, 128 * (hk % 16):128 * (hk % 16) + 128],
                           hb[:, hcol + TC * sub:hcol + TC * sub + TC],
                           hk == 0, False,
                           [f"w2t{hk // 16}", f"hkb{hk}"],
                           f"f2_{ly}_{th}_{f}_{sub}_{hk}")
                    MM(bank, PS[bank][:, :],
                       brow[64:65, 128 * f:128 * f + 128],
                       ones65[64:65, :], False, True,
                       ["brow2", "ones65"], f"f2b{ly}_{th}_{f}_{sub}")
                    VOP("tensor_tensor",
                        (x_in[f][:, tof + TC * sub:tof + TC * sub + TC],
                         x_in[f][:, tof + TC * sub:tof + TC * sub + TC],
                         PS[bank][:, :], ALU.add),
                        [f"x_in{f}", psr[bank]], [f"x_in{f}"],
                        f"f2r{ly}_{th}_{f}_{sub}")

        def emit_brow(ly):
            for rbase, rowi in ((0, 0), (32, 1), (64, 2)):
                DMA("gpsimd", brow[rbase:rbase + 1, :],
                    brow_h[rowi:rowi + 1, ly * D:(ly + 1) * D],
                    ["brow_h"], [f"brow{rowi}"], f"brow{ly}_{rowi}")

        def emit_final():
            fincol = L * PCL
            NFCG = FG // 128
            per_kb = KBC // HSF

            def hg_loc(fc, sub):
                if fc < per_kb:
                    hb, hcol = kbuf[0], fc * HSF
                elif fc < 2 * per_kb:
                    hb, hcol = kbuf[1], (fc - per_kb) * HSF
                else:
                    hb, hcol = vbuf[0], (fc - 2 * per_kb) * HSF
                return hb[:, hcol + TC * sub:hcol + TC * sub + TC], \
                    f"hkb{fc}"

            sch.barrier(sum((vb_regions(i) for i in range(2)), [])
                        + kb_regions(0) + kb_regions(1)
                        + [f"hkb{i}" for i in range(NFC1)]
                        + [f"qp{i}" for i in range(2)] + ["biasall"]
                        + ["w2t0", "w2t1"]
                        + [f"ofm{d}" for d in range(DCH)],
                        [f"xng{d}" for d in range(DCH)]
                        + [f"hkb{i}" for i in range(NFCG)]
                        + ["xtmf", "wu0", "wu1"])
            for th in range(NTH):
                tof = HSF * th
                emit_ln("fin%d" % th, th, wcol=fincol, bcol=fincol + 8)
                xng = []
                for dd in range(DCH):
                    ap = vbuf[1][:, HSF * dd:HSF * dd + HSF]
                    xng.append(ap)
                    DMA("sync", ap,
                        xnscr[128 * dd:128 * dd + 128, tof:tof + HSF],
                        ["xnscr"], [f"xng{dd}"], f"gxn{th}_{dd}")
                sch.barrier(["w2t0", "w2t1", "stf0", "stf1", "stf2"], ["wu0", "wu1"])
                for fc in range(NFCG):
                    wtg = wtile[fc % 4]
                    wtu = wtile2v[fc % 2]
                    DMA("gpsimd",
                        wtg[:, :].rearrange("p (c f) -> p c f", f=128),
                        wg_h[:, 128 * fc:128 * fc + 128]
                        .rearrange("(c p) f -> p c f", p=128),
                        ["wg_h"], [f"wt{fc % 4}"], f"wgld{th}_{fc}")
                    DMA("gpsimd",
                        wtu.rearrange("p (c f) -> p c f", f=128),
                        wu_h[:, 128 * fc:128 * fc + 128]
                        .rearrange("(c p) f -> p c f", p=128),
                        ["wu_h"], [f"wu{fc % 2}"], f"wuld{th}_{fc}")
                    for sub in range(NSUB):
                        rot = (fc if NSUB == 1 else sub) % 2
                        bg = PA + rot
                        bu = PC + rot
                        for dd in range(DCH):
                            MM(bg, PS[bg][:, :],
                               wtg[:, 128 * dd:128 * dd + 128],
                               xng[dd][:, TC * sub:TC * sub + TC],
                               dd == 0, dd == DCH - 1,
                               [f"wt{fc % 4}", f"xng{dd}"],
                               f"gg_{th}_{fc}_{sub}_{dd}")
                            MM(bu, PS[bu][:, :],
                               wtu[:, 128 * dd:128 * dd + 128],
                               xng[dd][:, TC * sub:TC * sub + TC],
                               dd == 0, dd == DCH - 1,
                               [f"wu{fc % 2}", f"xng{dd}"],
                               f"gu_{th}_{fc}_{sub}_{dd}")
                        sil = sqt[rot]
                        ACT(sil[:, :], PS[bg][:, :], AF.Silu,
                            [psr[bg]], [f"sqt{rot}"], scale=DEQ,
                            tag=f"gsil{th}_{fc}_{sub}")
                        upv = evac[rot]
                        VOP("tensor_copy", (upv, PS[bu][:, :]),
                            [psr[bu]], [f"evac{rot}"],
                            f"gup{th}_{fc}_{sub}")
                        hap, hreg = hg_loc(fc, sub)
                        VOP("tensor_tensor", (hap, sil[:, :], upv,
                                              ALU.mult),
                            [f"sqt{rot}", f"evac{rot}"], [hreg],
                            f"gh{th}_{fc}_{sub}")
                sch.barrier(["wu0", "wu1"], ["w2t0"])
                for f in range(DCH):
                    DMA("gpsimd",
                        wdtile[:, 0:128 * NFCG]
                        .rearrange("p (c f) -> p c f", f=128),
                        wd_h[:, 128 * f:128 * f + 128]
                        .rearrange("(c p) f -> p c f", p=128),
                        ["wd_h"], ["w2t0"], f"wdld{th}_{f}")
                    for sub in range(NSUB):
                        rot = (f if NSUB == 1 else sub) % 2
                        bank = PA + rot
                        for hk in range(NFCG):
                            hap, hreg = hg_loc(hk, sub)
                            MM(bank, PS[bank][:, :],
                               wdtile[:, 128 * hk:128 * hk + 128],
                               hap, hk == 0, hk == NFCG - 1,
                               ["w2t0", hreg],
                               f"gd{th}_{f}_{sub}_{hk}")
                        gt = evac[2 + rot]
                        ACT(gt, PS[bank][:, :], AF.Silu,
                            [psr[bank]], [f"evac{2 + rot}"],
                            scale=DEQ * DEQ, tag=f"gout{th}_{f}_{sub}")
                        tc = slice(tof + TC * sub, tof + TC * sub + TC)
                        ltc = slice(TC * sub, TC * sub + TC)
                        VOP("tensor_tensor",
                            (x_in[f][:, tc], xbf[f][:, ltc], gt,
                             ALU.mult),
                            [f"xbf{f}", f"evac{2 + rot}"], [f"x_in{f}"],
                            f"gc{th}_{f}_{sub}")
                for q in range(tof // 128, (tof + HSF) // 128):
                    DMA("sync", xtmf, x_h[128 * q:128 * q + 128, :],
                        ["x_h"], ["xtmf"], f"fxld{q}")
                    for dd in range(DCH):
                        bank = PA if dd < 4 else PB
                        TRANS(PS[bank][:, 128 * (dd % 4):
                                       128 * (dd % 4) + 128],
                              x_in[dd][:, 128 * q:128 * q + 128],
                              identf[:, :],
                              [f"x_in{dd}", "identf"], [psr[bank]],
                              f"ftr{q}_{dd}")
                    ot = outtm[q % 2]
                    VOP("tensor_tensor", (ot[:, 0:512], PS[PA][:, :],
                                          xtmf[:, 0:512], ALU.add),
                        [psr[PA], "xtmf"], [f"outtm{q % 2}"], f"fc0_{q}")
                    VOP("tensor_tensor", (ot[:, 512:1024], PS[PB][:, :],
                                          xtmf[:, 512:1024], ALU.add),
                        [psr[PB], "xtmf"], [f"outtm{q % 2}"], f"fc1_{q}")
                    DMA("sync", out_h[128 * q:128 * q + 128, :], ot[:, :],
                        [f"outtm{q % 2}"], ["out_h"], f"fst{q}")

        for ly in range(L):
            lc = ly * PCL
            emit_brow(ly)
            emit_qkv_weights(ly)
            for th in range(NTH):
                emit_ln(f"l{ly}a{th}", th, wcol=lc, bcol=lc + 8)
                emit_qkv_half(ly, th)
            emit_norms_ag(ly)
            emit_attention(ly)
            emit_outproj(ly)
            for th in range(NTH):
                emit_ln(f"l{ly}b{th}", th, wcol=lc + 16, bcol=lc + 24)
                emit_ffn_half(ly, th)
        sch.barrier([f"qp{i}" for i in range(2)] + ["oin0", "oin1"],
                    ["outtm0", "outtm1"])
        emit_final()
        sch.finalize(final_engine="sync")
    return nc


def _pack_params(g, ins):
    L, F = g.L, g.F
    PCL = 32 + 16 + F // 128
    cols = []

    def chunked(v):
        return np.ascontiguousarray(v.reshape(-1, 128).T)

    out = np.zeros((128, L * PCL + 16), dtype=np.float32)
    for ly in range(L):
        c = ly * PCL
        out[:, c:c + 8] = chunked(ins["L0_attn_ln1_w"][ly])
        out[:, c + 8:c + 16] = chunked(ins["L0_attn_ln1_b"][ly])
        out[:, c + 16:c + 24] = chunked(ins["L0_attn_ln2_w"][ly])
        out[:, c + 24:c + 32] = chunked(ins["L0_attn_ln2_b"][ly])
        out[:, c + 32:c + 48] = chunked(ins["L0_attn_in_proj_b"][ly][:2048])
        out[:, c + 48:c + 48 + F // 128] = chunked(ins["L0_attn_ffn1_b"][ly])
    out[:, L * PCL:L * PCL + 8] = chunked(ins["L0_attn_final_w"])
    out[:, L * PCL + 8:L * PCL + 16] = chunked(ins["L0_attn_final_b"])
    return out


def _host_tables(g, gi):
    flag = np.zeros((1, g.NBC), dtype=np.float32)
    for (h, s, u, i), c in g.bias_col.items():
        beta = g.Wq[h] // 128
        t = (4 * s + gi) * 4 + (u + 1) * beta - 1 - i
        if t < 0:
            flag[0, c] = 1.0
    idx = np.zeros((128, g.NIDX), dtype=np.uint32)
    for p, (kind, h, m) in enumerate(g.idx_plan):
        cg = m + (g.bmin[h] - g.padlo[h]) // 4 + gi
        cg = min(max(cg, 0), g.NCH - 1)
        r, sp = cg % 4, cg // 4
        blk, a = h // 2, h % 2
        rbase = blk * 4 * g.PBLK + r * g.PBLK
        if kind == "K":
            base = (rbase + 2 * g.VSZ + a * g.KSZ) // 512
            rows = base + np.arange(64) * (g.TLOC // 512) + sp
            idx[0:64, p] = rows
        else:
            base = (rbase + a * g.VSZ) // 260
            idx[:, p] = base + 128 * sp + np.arange(128)
    return flag, idx


_GEOM = None
_NC = None


def prepare_in_maps(g, ins):
    x = np.asarray(ins["L0_attn_x"], dtype=np.float32)
    wqkv = np.ascontiguousarray(
        np.asarray(ins["L0_attn_in_proj_w"], np.float32).transpose(0, 2, 1))
    wo = np.ascontiguousarray(
        np.asarray(ins["L0_attn_out_proj_w"], np.float32).transpose(0, 2, 1))
    w1 = np.ascontiguousarray(
        np.asarray(ins["L0_attn_ffn1_w"], np.float32).transpose(0, 2, 1))
    w2 = np.ascontiguousarray(
        np.asarray(ins["L0_attn_ffn2_w"], np.float32).transpose(0, 2, 1))
    wg = np.ascontiguousarray(np.asarray(ins["L0_ffn_w_gate"], np.int32).T)
    wu = np.ascontiguousarray(np.asarray(ins["L0_ffn_w_up"], np.int32).T)
    wd = np.ascontiguousarray(np.asarray(ins["L0_ffn_w_down"], np.int32).T)
    params = _pack_params(g, ins)
    brow = np.stack([
        np.ascontiguousarray(
            np.asarray(ins["L0_attn_in_proj_b"], np.float32)[:, 2048:3072]
        ).reshape(-1),
        np.asarray(ins["L0_attn_out_proj_b"], np.float32).reshape(-1),
        np.asarray(ins["L0_attn_ffn2_b"], np.float32).reshape(-1),
    ]).astype(np.float32)
    cpack, rampall = host_consts(g)
    in_maps = []
    for c in range(8):
        seq, gi = c // 4, c % 4
        slab = np.ascontiguousarray(np.concatenate(
            [x[seq, (4 * s + gi) * g.CH:(4 * s + gi + 1) * g.CH]
             for s in range(g.NS)], axis=0))
        flag, idx = _host_tables(g, gi)
        in_maps.append({
            "x_slab": slab, "wqkv": wqkv, "wo": wo, "w1": w1, "w2": w2,
            "wg": wg, "wu": wu, "wd": wd, "params": params, "brow": brow,
            "flagrow": flag, "idxtab": idx, "cpack": cpack,
            "rampall": rampall,
        })
    return in_maps


def assemble_out(g, slabs):
    out = np.zeros((2, g.T, g.D), dtype=np.float32)
    for c in range(8):
        seq, gi = c // 4, c % 4
        for s in range(g.NS):
            out[seq, (4 * s + gi) * g.CH:(4 * s + gi + 1) * g.CH] = \
                slabs[c][s * g.CH:(s + 1) * g.CH]
    return out


def kernel(**ins):
    global _GEOM, _NC
    g = _GEOM or Geom(T=8192, depth=4)
    _GEOM = g
    if _NC is None:
        _NC = build_program(g)
    in_maps = prepare_in_maps(g, ins)
    res = run_bass_kernel_spmd(_NC, in_maps, core_ids=list(range(8)))
    return assemble_out(g, [res.results[c]["out_slab"] for c in range(8)])


# version 16 (final)
# speedup vs baseline: 1.0367x; 1.0050x over previous
import math
from contextlib import ExitStack

import numpy as np
import concourse.bass as bass
import concourse.mybir as mybir
from concourse.bass_utils import run_bass_kernel_spmd

F32 = mybir.dt.float32
BF16 = mybir.dt.bfloat16
I32 = mybir.dt.int32
U32 = mybir.dt.uint32
AF = mybir.ActivationFunctionType
ALU = mybir.AluOpType
AX = mybir.AxisListType

LN_EPS = 1e-5
DEQ = 1.0 / 127.0
M_MARGIN = 3.0
SKIP_MARGIN = 88.0
WQ_CAP = 64.0
KILL = 30000.0


def alibi_slopes(h):
    closest = 2 ** math.floor(math.log2(h))
    base = 2 ** (-(2 ** -(math.log2(closest) - 3)))
    slopes = base ** np.arange(1, closest + 1)
    if closest != h:
        eb = 2 ** (-(2 ** -(math.log2(2 * closest) - 3)))
        extra = eb ** np.arange(1, 2 * (h - closest) + 1, 2)
        slopes = np.concatenate([slopes, extra])
    return slopes.astype(np.float64)


class Geom:
    def __init__(self, T=8192, depth=4):
        self.B, self.T, self.D, self.H, self.HD = 2, T, 1024, 16, 64
        self.L, self.F, self.FG = depth, 4096, 3072
        self.NC, self.GRP = 8, 4
        self.CH = 512
        assert T % (4 * self.CH) == 0
        self.NCH = T // self.CH
        self.NS = self.NCH // 4
        self.TLOC = self.NS * self.CH
        self.NKT = T // 128
        self.NQS = self.TLOC // 128
        self.DCH = self.D // 128
        self.TC = 512
        self.NT = self.TLOC // self.TC
        self.slopes = alibi_slopes(self.H)
        self.Wq = []
        for sl in self.slopes:
            w = 128
            while w < min(512, self.CH) and sl * (2 * w - 1) <= WQ_CAP:
                w *= 2
            self.Wq.append(w)
        self.nslot = [
            min(self.NKT, int((SKIP_MARGIN / sl + w - 1) // 128) + 1)
            for sl, w in zip(self.slopes, self.Wq)
        ]
        self.nmat = {}
        for h in range(self.H):
            w = self.Wq[h]
            for s in range(self.NS):
                for u in range(self.CH // w):
                    caus = ((4 * s + 3) * self.CH + (u + 1) * w) // 128
                    self.nmat[(h, s, u)] = min(self.nslot[h], caus)
        self.bmin, self.nbuf = {}, {}
        for h in range(self.H):
            beta = self.Wq[h] // 128
            bm, bx = 10 ** 9, -10 ** 9
            for s in range(self.NS):
                w = self.Wq[h]
                for u in range(self.CH // w):
                    n = self.nmat[(h, s, u)]
                    bm = min(bm, 16 * s + (u + 1) * (w // 128) - n)
                    bx = max(bx, 16 * s + (u + 1) * (w // 128) - 1)
            self.bmin[h] = bm
            self.nbuf[h] = bx - bm + 1
        border = sorted(range(self.H // 2),
                        key=lambda b: (-max(self.nbuf[2 * b],
                                            self.nbuf[2 * b + 1]), b))
        self.pairs = [(2 * b, 2 * b + 1) for b in border]
        self.bias_col = {}
        nb = 0
        for h in range(self.H):
            w = self.Wq[h]
            for s in range(self.NS):
                for u in range(self.CH // w):
                    for i in range(self.nmat[(h, s, u)]):
                        self.bias_col[(h, s, u, i)] = nb
                        nb += 1
        self.NBC = nb
        for (ha, hb) in self.pairs:
            assert ha // 2 == hb // 2, "pair must map to one AG block"
        self.VROW = 65
        self.VSZ = self.TLOC * self.VROW
        self.KSZ = 64 * self.TLOC
        lcm = 33280
        raw = 2 * (self.VSZ + self.KSZ)
        self.PBLK = ((raw + lcm - 1) // lcm) * lcm
        self.SHARD = 8 * self.PBLK
        assert self.PBLK % 512 == 0 and self.PBLK % 260 == 0
        assert self.VSZ % 260 == 0 and (2 * self.VSZ) % 512 == 0
        assert self.KSZ % 512 == 0
        self.padlo = {}
        self.kload = {}
        self.idx_plan = []
        for h in range(self.H):
            al = (-self.bmin[h]) % 4
            self.padlo[h] = (4 - al) % 4 if al != 0 else 0
            nb_tiles = self.nbuf[h] + self.padlo[h]
            ngrp = (nb_tiles + 3) // 4
            self.kload[h] = []
            for m in range(ngrp):
                self.idx_plan.append(("K", h, m))
                self.kload[h].append((4 * m, len(self.idx_plan) - 1))
            self.vload_base = None
        self.vload = {}
        for h in range(self.H):
            ngrp = (self.nbuf[h] + self.padlo[h] + 3) // 4
            self.vload[h] = []
            for m in range(ngrp):
                self.idx_plan.append(("V", h, m))
                self.vload[h].append((4 * m, len(self.idx_plan) - 1))
        self.NIDX = len(self.idx_plan)
        self.NBK = max(4 * ((self.nbuf[h] + self.padlo[h] + 3) // 4)
                       for h in range(self.H))

    def bslot(self, h, s, u, i):
        beta = self.Wq[h] // 128
        return 16 * s + (u + 1) * beta - 1 - i - self.bmin[h] + self.padlo[h]

    def mask_cols(self):
        return [(512, i) for i in range(4)]


def host_consts(g):
    mc = g.mask_cols()
    CC = sum(w for w, i in mc) + 128 + 8 * 16 + 1
    cp = np.zeros((128, CC), dtype=np.float32)
    p = np.arange(128)
    off = 0
    for (w, i) in mc:
        q = np.arange(w)[None, :]
        cp[:, off:off + w] = (q >= (p[:, None] + w - 128 * (i + 1)))
        off += w
    cp[:, off:off + 128] = np.eye(128, dtype=np.float32)
    off += 128
    for c in range(8):
        blk = np.zeros((128, 16), dtype=np.float32)
        for pp in range(128):
            blk[pp, 2 * c + (1 if pp >= 64 else 0)] = 1.0
        cp[:, off + 16 * c: off + 16 * (c + 1)] = blk
    off += 128
    cp[:, off] = p
    ramp = np.zeros((128, g.NBC), dtype=np.float32)
    for (h, s, u, i), col in g.bias_col.items():
        w = g.Wq[h]
        ramp[:, col] = g.slopes[h] * (p + w - 128 * (i + 1))
    return cp, ramp


class _Op:
    __slots__ = ("idx", "engine", "emit", "reads", "writes", "deps",
                 "need_inc", "inc_value", "inc_amt", "is_dma", "queue", "tag")

    def __init__(self, idx, engine, emit, reads, writes, is_dma, queue, tag,
                 inc_amt):
        self.idx, self.engine, self.emit = idx, engine, emit
        self.reads, self.writes = reads, writes
        self.deps = set()
        self.need_inc, self.inc_value = False, None
        self.inc_amt = inc_amt
        self.is_dma, self.queue, self.tag = is_dma, queue, tag


_COMPUTE = ("tensor", "vector", "scalar", "gpsimd")
_QUEUES = ("syncq", "gpsq", "scalq", "ccq")
_QOF = {"sync": "syncq", "gpsimd": "gpsq", "scalar": "scalq"}


class Sched:
    def __init__(self, nc):
        self.nc = nc
        self.ops = []
        self.last_writer = {}
        self.readers = {}
        self.pending = {}

    def add(self, engine, emit, reads=(), writes=(), dma=False, cc=False,
            tag=""):
        queue = "ccq" if cc else (_QOF[engine] if dma else None)
        op = _Op(len(self.ops), engine, emit, list(reads), list(writes),
                 dma or cc, queue, tag, 1 if cc else (16 if dma else 1))
        for r in op.reads:
            w = self.last_writer.get(r)
            if w is not None:
                op.deps.add(w)
        for w in op.writes:
            pw = self.last_writer.get(w)
            if pw is not None:
                op.deps.add(pw)
            for rd in self.readers.get(w, ()):
                op.deps.add(rd)
            if w in self.pending:
                op.deps |= self.pending.pop(w)
        op.deps.discard(op.idx)
        self.ops.append(op)
        for r in op.reads:
            self.readers.setdefault(r, []).append(op.idx)
        for w in op.writes:
            self.last_writer[w] = op.idx
            self.readers[w] = []
        return op.idx

    def barrier(self, old_regions, new_regions):
        dep = set()
        for r in old_regions:
            if r in self.last_writer:
                dep.add(self.last_writer[r])
            dep.update(self.readers.get(r, ()))
        for r in new_regions:
            self.pending.setdefault(r, set()).update(dep)

    DMA_LANES = 6

    def finalize(self, final_engine="sync"):
        nc, ops = self.nc, self.ops
        for op in ops:
            for d in op.deps:
                ops[d].need_inc = True
        for op in ops:
            if op.is_dma:
                op.need_inc = True
        lane_count = {}
        for op in ops:
            if op.is_dma and op.queue != "ccq":
                j = lane_count.get(op.queue, 0)
                lane_count[op.queue] = j + 1
                op.queue = f"{op.queue}{j % self.DMA_LANES}"
        counters = {}
        for op in ops:
            dom = op.queue if op.is_dma else op.engine
            if op.need_inc:
                counters[dom] = counters.get(dom, 0) + op.inc_amt
                op.inc_value = counters[dom]
        self.final_counts = dict(counters)

        with ExitStack() as st:
            sem_names = list(_COMPUTE) + ["ccq"] + [
                f"{q}{r}" for q in ("syncq", "gpsq", "scalq")
                for r in range(self.DMA_LANES)]
            sems = {n: st.enter_context(nc.semaphore("sem_" + n))
                    for n in sem_names}
            block = st.enter_context(nc.Block())
            by_eng = {e: [] for e in
                      ("tensor", "vector", "scalar", "gpsimd", "sync")}
            for op in ops:
                by_eng[op.engine].append(op)
            wm = {}

            def body_for(ename, elist):
                def body(eng):
                    for op in elist:
                        tgt = {}
                        for d in op.deps:
                            p = ops[d]
                            dom = p.queue if p.is_dma else p.engine
                            if (not p.is_dma) and p.engine == op.engine \
                                    and p.engine == "tensor":
                                continue
                            v = p.inc_value
                            if wm.get((ename, dom), 0) >= v:
                                continue
                            tgt[dom] = max(tgt.get(dom, 0), v)
                        if op.is_dma and op.queue != "ccq" \
                                and op.inc_value > 16:
                            dom = op.queue
                            v = op.inc_value - 16
                            if wm.get((ename, dom), 0) < v:
                                tgt[dom] = max(tgt.get(dom, 0), v)
                        for dom in sorted(tgt):
                            eng.wait_ge(sems[dom], tgt[dom])
                            wm[(ename, dom)] = tgt[dom]
                        inst = op.emit(eng)
                        if op.need_inc:
                            dom = op.queue if op.is_dma else op.engine
                            assert inst is not None, f"op needs inst: {op.tag}"
                            inst.then_inc(sems[dom], op.inc_amt)
                    if ename == final_engine:
                        for dom in sorted(counters):
                            eng.wait_ge(sems[dom], counters[dom])
                return body

            for ename in ("sync", "tensor", "vector", "scalar", "gpsimd"):
                getattr(block, ename)(body_for(ename, by_eng[ename]))


def build_program(g: Geom):
    nc = bass.Bass("TRN2", target_bir_lowering=False, debug=False)
    D, H, L, F, FG = g.D, g.H, g.L, g.F, g.FG
    TLOC, TC, NT, NS, DCH, NQS = g.TLOC, g.TC, g.NT, g.NS, g.DCH, g.NQS
    NBK = g.NBK
    HSF = min(1024, TLOC)
    NTH = TLOC // HSF
    NSUB = HSF // TC

    x_h = nc.dram_tensor("x_slab", [TLOC, D], F32, kind="ExternalInput").ap()
    wqkv_h = nc.dram_tensor("wqkv", [L, D, 3 * D], F32, kind="ExternalInput").ap()
    wo_h = nc.dram_tensor("wo", [L, D, D], F32, kind="ExternalInput").ap()
    w1_h = nc.dram_tensor("w1", [L, D, F], F32, kind="ExternalInput").ap()
    w2_h = nc.dram_tensor("w2", [L, F, D], F32, kind="ExternalInput").ap()
    wg_h = nc.dram_tensor("wg", [D, FG], I32, kind="ExternalInput").ap()
    wu_h = nc.dram_tensor("wu", [D, FG], I32, kind="ExternalInput").ap()
    wd_h = nc.dram_tensor("wd", [FG, D], I32, kind="ExternalInput").ap()
    PCL = 32 + 16 + F // 128
    PCOLS = L * PCL + 16
    params_h = nc.dram_tensor("params", [128, PCOLS], F32,
                              kind="ExternalInput").ap()
    brow_h = nc.dram_tensor("brow", [3, L * D], F32, kind="ExternalInput").ap()
    flag_h = nc.dram_tensor("flagrow", [1, g.NBC], F32,
                            kind="ExternalInput").ap()
    idx_h = nc.dram_tensor("idxtab", [128, g.NIDX], U32,
                           kind="ExternalInput").ap()
    mc = g.mask_cols()
    CC = sum(w for w, i in mc) + 128 + 8 * 16 + 1
    cpack_h = nc.dram_tensor("cpack", [128, CC], F32,
                             kind="ExternalInput").ap()
    rampall_h = nc.dram_tensor("rampall", [128, g.NBC], F32,
                               kind="ExternalInput").ap()
    out_h = nc.dram_tensor("out_slab", [TLOC, D], F32,
                           kind="ExternalOutput").ap()

    agin = nc.dram_tensor("agin", [g.SHARD], BF16).ap()
    agout = nc.dram_tensor("agout", [4 * g.SHARD], BF16).ap()
    qscr = nc.dram_tensor("q_scr", [D, TLOC], BF16).ap()
    oscr = nc.dram_tensor("o_scr", [NQS, 128, D], BF16).ap()
    xnscr = nc.dram_tensor("xn_scr", [D, TLOC], BF16).ap()
    rowscr = nc.dram_tensor("row_scr", [64], F32).ap()
    agin_st = nc.dram_tensor("agin_st", [64], BF16).ap()
    agout_st = nc.dram_tensor("agout_st", [256], BF16).ap()
    agout_k = agout.rearrange("(r c) -> r c", c=512)
    agout_v = agout.rearrange("(r c) -> r c", c=260)

    groups = [[0, 1, 2, 3], [4, 5, 6, 7]]
    sch = Sched(nc)

    with ExitStack() as stx:
        def sb(name, shape, dt):
            return stx.enter_context(nc.sbuf_tensor(name, shape, dt))

        x_in = [sb(f"x_in{d}", [128, TLOC], F32) for d in range(DCH)]
        xbf = [sb(f"xbf{d}", [128, HSF], BF16) for d in range(DCH)]
        params = sb("params_sb", [128, PCOLS], F32)
        brow = sb("brow_sb", [65, D], BF16)
        onesrow = sb("onesrow", [1, TC], BF16)
        ones65 = sb("ones65", [65, TC], BF16)
        idxtab = sb("idxtab_sb", [128, g.NIDX], U32)
        flagst = sb("flagst", [1, 512], BF16)
        mrowst = sb("mrowst", [1, 512], BF16)
        biasall = sb("biasall", [128, g.NBC], F32)
        rows = sb("rows_sb", [1, 256], F32)
        mask512 = [sb(f"mask_512_{i}", [128, 512], BF16) for i in range(4)]
        masks = {}
        mask_reg = {}
        for w in (128, 256, 512):
            for i in range(w // 128):
                j = i + (512 - w) // 128
                masks[(w, i)] = mask512[j][:, 0:w]
                mask_reg[(w, i)] = f"mask512_{j}"
        identb = sb("identb", [128, 128], BF16)
        identf = sb("identf", [128, 128], F32)
        blkind = [sb(f"blkind{c}", [128, 16], BF16) for c in range(DCH)]
        piota = sb("piota", [128, 1], F32)
        allones = sb("allones", [128, 128], BF16)
        sqt = [sb(f"sqt{i}", [128, TC], BF16) for i in range(2)]
        evac_t = sb("evac_t", [128, 4 * TC], BF16)
        ebuf_t = sb("ebuf_t", [128, 6 * 512], BF16)
        mu_bc = [sb(f"mu_bc{t}", [128, TC], BF16) for t in range(2)]
        r_bc = [sb(f"r_bc{t}", [128, TC], BF16) for t in range(2)]
        stat_ft = sb("stat_ft", [128, 3 * TC], F32)
        stat_f = [stat_ft[:, TC * i:TC * (i + 1)] for i in range(3)]
        evac = [evac_t[:, TC * i:TC * (i + 1)] for i in range(4)]
        vstage = sb("vstage", [128, 520], BF16)
        KBC = max(NBK * 128, 8 * 1024, 4 * TLOC)
        VBC = max(2 * NBK * 65, 8 * HSF, 8 * 1024)
        kbuf = [sb(f"kbuf{i}", [128, KBC], BF16) for i in range(2)]
        vbuf = [sb(f"vbuf{i}", [128, VBC], BF16) for i in range(2)]
        qpair = [sb(f"qpair{i}", [128, max(TLOC, 2 * D)], BF16)
                 for i in range(2)]
        ebuf = [ebuf_t[:, 512 * i:512 * (i + 1)] for i in range(6)]
        rcpb = [sb(f"rcp{i}", [128, 1], F32) for i in range(4)]
        otm = [sb(f"otm{i}", [128, 512], BF16) for i in range(2)]
        maxsq = [sb(f"maxsq{i}", [16, 1], F32) for i in range(2)]
        red16 = sb("red16", [16, 1], F32)
        col1 = [sb(f"col1_{i}", [128, 1], F32) for i in range(6)]
        wtile = [sb(f"wtile{i}", [128, 1024], BF16) for i in range(4)]
        if g.NBC >= 1024:
            w2half = [biasall[:, 0:1024].bitcast(BF16),
                      stat_ft[:, 0:1024].bitcast(BF16)]
        else:
            w2h_real = sb("w2h_real", [128, 2048], BF16)
            w2half = [w2h_real[:, :], stat_ft[:, 0:1024].bitcast(BF16)]
        wdtile = stat_ft[:, :].bitcast(BF16)
        wtile2v = [wdtile[:, 0:1024], wdtile[:, 1024:2048]]
        xtm = evac_t[:, 0:2 * D].bitcast(F32)
        xsq = ebuf_t[:, 0:2 * D].bitcast(F32)
        if g.NBC * 4 >= 4 * D:
            xtmf = biasall[:, 0:D].bitcast(F32) if False else biasall[:, 0:D]
        else:
            xtmf = sb("xtmf", [128, D], F32)[:, :]
        outtm = [qpair[i][:, 0:2 * D].bitcast(F32) for i in range(2)]
        PS = [stx.enter_context(nc.psum_tensor(f"ps{i}", [128, 512], F32))
              for i in range(7)]
        TRB = stx.enter_context(nc.psum_tensor("trb", [128, 1024], BF16))
        S0, S1, PA, PB, PC, PD, PM = range(7)
        psr = [f"ps{i}" for i in range(7)] + ["trb"]

        def DMA(engine, out, in_, reads, writes, tag="", slow=False):
            def f(eng):
                return eng.dma_start(out=out, in_=in_,
                                     allow_slow_non_contiguous=slow)
            return sch.add(engine, f, reads=reads, writes=writes, dma=True,
                           tag=tag)

        def MM(bank, psum_ap, lhsT, rhs, start, stop, reads, tag=""):
            def f(eng):
                return eng.matmul(psum_ap, lhsT, rhs, start=start, stop=stop)
            return sch.add("tensor", f, reads=reads, writes=[psr[bank]],
                           tag=tag)

        def TRANS(out_ap, in_ap, ident, reads, writes, tag=""):
            def f(eng):
                return eng.transpose(out_ap, in_ap, ident)
            return sch.add("tensor", f, reads=reads, writes=writes, tag=tag)

        def ACT(out, in_, func, reads, writes, bias=None, scale=None, tag=""):
            kw = {}
            if bias is not None:
                kw["bias"] = bias
            if scale is not None:
                kw["scale"] = scale

            def f(eng):
                return eng.activation(out, in_, func, **kw)
            return sch.add("scalar", f, reads=reads, writes=writes, tag=tag)

        def VOP(name, args, reads, writes, tag="", engine="vector"):
            def f(eng):
                return getattr(eng, name)(*args)
            return sch.add(engine, f, reads=reads, writes=writes, tag=tag)

        def GATHER(out, in_, idxcol, reads, writes, tag=""):
            def f(eng):
                return eng.indirect_dma_start(
                    out=out, out_offset=None, in_=in_,
                    in_offset=bass.IndirectOffsetOnAxis(idxcol, 0))
            return sch.add("gpsimd", f, reads=reads, writes=writes, dma=True,
                           tag=tag)

        def COLL(kind, op, ins, outs, reads, writes, tag="cc"):
            def f(eng):
                return eng.collective_compute(kind, op, replica_groups=groups,
                                              ins=ins, outs=outs)
            return sch.add("gpsimd", f, reads=reads, writes=writes, cc=True,
                           tag=tag)

        def kb_regions(i):
            return [f"kb{i}g{m}" for m in range((KBC // 128 + 3) // 4)]

        def vb_regions(i):
            return [f"vb{i}a{a}g{m}" for a in range(2)
                    for m in range((NBK + 3) // 4)]

        NFC1 = F // 128

        def h_slot(fc):
            per_kb = KBC // HSF
            per_vb = VBC // HSF
            if fc < per_kb:
                return kbuf[0], fc * HSF
            fc -= per_kb
            if fc < per_kb:
                return kbuf[1], fc * HSF
            fc -= per_kb
            if fc < per_vb:
                return vbuf[0], fc * HSF
            fc -= per_vb
            return vbuf[1], fc * HSF

        DMA("sync", params[:, :], params_h, ["params_h"], ["params"], "params")
        DMA("sync", idxtab[:, :], idx_h, ["idx_h"], ["idxtab"], "idxtab")
        off = 0
        for (w, i) in mc:
            j = i + (512 - w) // 128
            DMA("gpsimd", mask512[j][:, :], cpack_h[:, off:off + w],
                ["cpack_h"], [f"mask512_{j}"], f"mask512_{j}")
            off += w
        DMA("gpsimd", identb[:, :], cpack_h[:, off:off + 128],
            ["cpack_h"], ["identb"], "identb")
        DMA("sync", identf[:, :], cpack_h[:, off:off + 128],
            ["cpack_h"], ["identf"], "identf")
        off += 128
        for c in range(DCH):
            DMA("gpsimd", blkind[c][:, :], cpack_h[:, off:off + 16],
                ["cpack_h"], [f"blkind{c}"], f"blkind{c}")
            off += 16
        DMA("sync", piota[:, :], cpack_h[:, off:off + 1],
            ["cpack_h"], ["piota"], "piota", slow=True)
        VOP("memset", (onesrow[:, :], 1.0), [], ["onesrow"], "ones")
        VOP("memset", (allones[:, :], 1.0), [], ["allones"], "allones")
        VOP("memset", (ones65[:, :], 1.0), [], ["ones65"], "ones65")
        VOP("memset", (vstage[:, :], 1.0), [], ["vstage"], "vst_init")
        zview = kbuf[0][:, 0:4096]
        VOP("memset", (zview, 0.0), [], ["zbuf"], "zbuf")
        DMA("sync", bass.AP(agin_st.tensor, 0, [[1, 64]]),
            zview[0:1, 0:64], ["zbuf"], ["agin_s"], "zagst")
        ZCH = 128 * 4096
        nz = (g.SHARD + ZCH - 1) // ZCH
        for zi in range(nz):
            n = min(ZCH, g.SHARD - zi * ZCH)
            zrows = n // 4096
            rem = n - zrows * 4096
            if zrows:
                DMA("sync",
                    bass.AP(agin.tensor, zi * ZCH, [[4096, zrows], [1, 4096]]),
                    zview[0:zrows, :], ["zbuf"],
                    [f"aginb{b}" for b in range(8)] + ["agin_s"], f"zag{zi}")
            if rem:
                DMA("sync",
                    bass.AP(agin.tensor, zi * ZCH + zrows * 4096, [[1, rem]]),
                    zview[0:1, 0:rem], ["zbuf"],
                    [f"aginb{b}" for b in range(8)] + ["agin_s"], f"zagr{zi}")

        recD = 1.0 / D
        for q in range(NQS):
            DMA("sync", xtm, x_h[128 * q:128 * q + 128, :],
                ["x_h"], ["xtm"], f"xld{q}")
            ACT(xsq, xtm, AF.Square, ["xtm"], ["xsq"],
                tag=f"xsq{q}")
            VOP("reduce_sum", (col1[0][:, :], xtm, AX.X),
                ["xtm"], ["col0"], f"xsum{q}")
            VOP("reduce_sum", (col1[1][:, :], xsq, AX.X),
                ["xsq"], ["col1"], f"xss{q}")
            VOP("tensor_scalar", (col1[0][:, :], col1[0][:, :], recD, None,
                                  ALU.mult), ["col0"], ["col0"], f"mu{q}")
            VOP("tensor_scalar", (col1[1][:, :], col1[1][:, :], recD, None,
                                  ALU.mult), ["col1"], ["col1"], f"ms{q}")
            VOP("tensor_tensor", (col1[2][:, :], col1[0][:, :], col1[0][:, :],
                                  ALU.mult), ["col0"], ["col2"], f"mu2{q}")
            VOP("tensor_tensor", (col1[1][:, :], col1[1][:, :], col1[2][:, :],
                                  ALU.subtract), ["col1", "col2"], ["col1"],
                f"var{q}")
            VOP("tensor_scalar", (col1[1][:, :], col1[1][:, :], LN_EPS,
                                  None, ALU.add), ["col1"], ["col1"],
                f"veps{q}")
            ACT(col1[2][:, :], col1[1][:, :], AF.Sqrt, ["col1"], ["col2"],
                tag=f"sd{q}")
            VOP("reciprocal", (col1[3][:, :], col1[2][:, :]),
                ["col2"], ["col3"], f"rstd{q}")
            VOP("tensor_scalar", (xtm, xtm, col1[0][:, 0:1],
                                  col1[3][:, 0:1], ALU.subtract, ALU.mult),
                ["xtm", "col0", "col3"], ["xtm"], f"xn{q}")
            for dd in range(DCH):
                bank = PA + (dd % 2)
                TRANS(PS[bank][:, 0:128], xtm[:, 128 * dd:128 * dd + 128],
                      identf[:, :], ["xtm", "identf"], [psr[bank]],
                      f"xtr{q}_{dd}")
                VOP("tensor_copy",
                    (x_in[dd][:, 128 * q:128 * q + 128], PS[bank][:, 0:128]),
                    [psr[bank]], [f"x_in{dd}"], f"xev{q}_{dd}")
        for dd in range(DCH):
            DMA("gpsimd", xnscr[128 * dd:128 * dd + 128, :], x_in[dd][:, :],
                [f"x_in{dd}"], ["xnscr"], f"xnscr{dd}")

        def emit_ln(tagp, th, wcol=None, bcol=None):
            sch.barrier(["w2t0", "w2t1", "wu0", "wu1"],
                        ["stf0", "stf1", "stf2"])
            tof = HSF * th
            for dd in range(DCH):
                VOP("tensor_copy", (xbf[dd][:, :],
                                    x_in[dd][:, tof:tof + HSF]),
                    [f"x_in{dd}"], [f"xbf{dd}"], f"{tagp}cast{dd}")
            for t in range(NSUB):
                tc = slice(TC * t, TC * (t + 1))
                tb = t % 2
                for dd in range(DCH):
                    sq = sqt[dd % 2]
                    ACT(sq[:, :], xbf[dd][:, tc], AF.Square,
                        [f"xbf{dd}"], [f"sqt{dd % 2}"], tag=f"{tagp}sq{t}_{dd}")
                    MM(PM, PS[PM][:, :], allones[:, :], xbf[dd][:, tc],
                       dd == 0, dd == DCH - 1, ["allones", f"xbf{dd}"],
                       f"{tagp}sum{t}_{dd}")
                    MM(PA, PS[PA][:, :], allones[:, :], sq[:, :],
                       dd == 0, dd == DCH - 1, ["allones", f"sqt{dd % 2}"],
                       f"{tagp}ss{t}_{dd}")
                VOP("tensor_scalar", (mu_bc[tb][:, :], PS[PM][:, :], recD,
                                      None, ALU.mult), [psr[PM]],
                    [f"mu_bc{tb}"], f"{tagp}mu{t}")
                VOP("tensor_scalar", (stat_f[0], PS[PM][:, :], recD,
                                      None, ALU.mult), [psr[PM]], ["stf0"],
                    f"{tagp}muf{t}")
                VOP("tensor_scalar", (stat_f[1], PS[PA][:, :], recD,
                                      None, ALU.mult), [psr[PA]], ["stf1"],
                    f"{tagp}ssf{t}")
                VOP("tensor_tensor", (stat_f[2], stat_f[0],
                                      stat_f[0], ALU.mult),
                    ["stf0"], ["stf2"], f"{tagp}mu2{t}")
                VOP("tensor_tensor", (stat_f[1], stat_f[1],
                                      stat_f[2], ALU.subtract),
                    ["stf1", "stf2"], ["stf1"], f"{tagp}var{t}")
                VOP("tensor_scalar", (stat_f[1], stat_f[1],
                                      LN_EPS, None, ALU.add),
                    ["stf1"], ["stf1"], f"{tagp}veps{t}")
                ACT(stat_f[2], stat_f[1], AF.Sqrt,
                    ["stf1"], ["stf2"], tag=f"{tagp}sd{t}")
                VOP("reciprocal", (stat_f[0], stat_f[2]),
                    ["stf2"], ["stf0"], f"{tagp}rr{t}")
                VOP("tensor_copy", (r_bc[tb][:, :], stat_f[0]),
                    ["stf0"], [f"r_bc{tb}"], f"{tagp}rbc{t}")
            for dd in range(DCH):
                for t in range(NSUB):
                    tc = slice(TC * t, TC * (t + 1))
                    tb = t % 2
                    VOP("tensor_tensor", (xbf[dd][:, tc], xbf[dd][:, tc],
                                          mu_bc[tb][:, :], ALU.subtract),
                        [f"xbf{dd}", f"mu_bc{tb}"], [f"xbf{dd}"],
                        f"{tagp}ap1_{dd}_{t}")
                    VOP("tensor_tensor", (xbf[dd][:, tc], xbf[dd][:, tc],
                                          r_bc[tb][:, :], ALU.mult),
                        [f"xbf{dd}", f"r_bc{tb}"], [f"xbf{dd}"],
                        f"{tagp}ap2_{dd}_{t}")
                    if wcol is not None:
                        VOP("tensor_scalar",
                            (xbf[dd][:, tc], xbf[dd][:, tc],
                             params[:, wcol + dd:wcol + dd + 1],
                             params[:, bcol + dd:bcol + dd + 1],
                             ALU.mult, ALU.add),
                            [f"xbf{dd}", "params"], [f"xbf{dd}"],
                            f"{tagp}ap3_{dd}_{t}")


        def emit_qkv_weights(ly):
            sch.barrier(["w2t0", "w2t1", "wu0", "wu1"],
                        ["stf0", "stf1", "stf2", "biasall"])
            sch.barrier(kb_regions(0) + kb_regions(1) + vb_regions(0)
                        + vb_regions(1) + [f"hkb{i}" for i in range(NFC1)]
                        + ["zbuf", "xtm", "xsq", "w2t0", "w2t1", "wu0",
                           "wu1"]
                        + [f"ofm{d}" for d in range(DCH)],
                        [f"qkw{fc}" for fc in range(16)]
                        + [f"vw{dd}" for dd in range(DCH)])
            for fc in range(16):
                dst = kbuf[fc // 8][:, 1024 * (fc % 8):1024 * (fc % 8 + 1)]
                DMA("gpsimd",
                    dst.rearrange("p (c f) -> p c f", f=128),
                    wqkv_h[ly, :, 128 * fc:128 * fc + 128]
                    .rearrange("(c p) f -> p c f", p=128),
                    ["wqkv_h"], [f"qkw{fc}"], f"qkwld{ly}_{fc}")
            for dd in range(DCH):
                dst = vbuf[0][:, 1024 * dd:1024 * (dd + 1)]
                DMA("gpsimd", dst,
                    wqkv_h[ly, 128 * dd:128 * dd + 128, 2048:3072],
                    ["wqkv_h"], [f"vw{dd}"], f"vwld{ly}_{dd}")

        def emit_qkv_half(ly, th):
            tof = HSF * th
            qkbias0 = ly * PCL + 32
            if th == 0:
                for which in (0, 1):
                    VOP("memset", (maxsq[which][:, :], 0.0), [],
                        [f"maxsq{which}"], f"nmz{ly}_{which}")
            for fc in range(16):
                wt = kbuf[fc // 8][:, 1024 * (fc % 8):1024 * (fc % 8 + 1)]
                for t in range(NSUB):
                    bank = PB + ((fc if NSUB == 1 else t) % 2)
                    tc = slice(TC * t, TC * (t + 1))
                    gtc = slice(tof + TC * t, tof + TC * t + TC)
                    for dd in range(DCH):
                        MM(bank, PS[bank][:, :],
                           wt[:, 128 * dd:128 * dd + 128], xbf[dd][:, tc],
                           dd == 0, dd == DCH - 1,
                           [f"qkw{fc}", f"xbf{dd}"], f"qk{ly}_{th}_{fc}_{t}_{dd}")
                    ev = evac[t % 4]
                    ACT(ev, PS[bank][:, :], AF.Identity,
                        [psr[bank], "params"], [f"evac{t % 4}"],
                        bias=params[:, qkbias0 + fc:qkbias0 + fc + 1],
                        scale=1.0, tag=f"qkev{ly}_{th}_{fc}_{t}")
                    sq = sqt[(fc + t) % 2]
                    ACT(sq[:, :], ev, AF.Square, [f"evac{t % 4}"],
                        [f"sqt{(fc + t) % 2}"], tag=f"qnsq{ly}_{th}_{fc}_{t}")
                    nbank = PM if t == 0 else PD
                    MM(nbank, PS[nbank][0:16, :], blkind[fc % 8][:, :],
                       sq[:, :], fc % 8 == 0, fc % 8 == 7,
                       [f"blkind{fc % 8}", f"sqt{(fc + t) % 2}"],
                       f"qnmm{ly}_{th}_{fc}_{t}")
                    if fc % 8 == 7:
                        which = 0 if fc < 8 else 1
                        VOP("reduce_max", (red16[:, :], PS[nbank][0:16, :],
                                           AX.X), [psr[nbank]], ["red16"],
                            f"qnred{ly}_{th}_{which}_{t}")
                        VOP("tensor_tensor",
                            (maxsq[which][:, :], maxsq[which][:, :],
                             red16[:, :], ALU.max),
                            [f"maxsq{which}", "red16"], [f"maxsq{which}"],
                            f"qnmax{ly}_{th}_{which}_{t}")
                    if fc < 8:
                        DMA("sync", qscr[128 * fc:128 * fc + 128, gtc],
                            ev, [f"evac{t % 4}"], ["qscr"],
                            f"qst{ly}_{th}_{fc}_{t}")
                    else:
                        blk = fc - 8
                        ko = blk * g.PBLK + 2 * g.VSZ
                        dstap = bass.AP(agin.tensor, ko + tof + TC * t,
                                        [[TLOC, 128], [1, TC]])
                        DMA("sync", dstap, ev,
                            [f"evac{t % 4}"], [f"aginb{blk}"],
                            f"kst{ly}_{th}_{fc}_{t}")
            vb0 = ly * D
            for tt in range(tof // 128, (tof + HSF) // 128):
                ltt = tt - tof // 128
                for dd in range(DCH):
                    lhs = xbf[dd][:, 128 * ltt:128 * ltt + 128]
                    for half, bank in ((0, PA), (1, PC)):
                        MM(bank, PS[bank][:, :], lhs,
                           vbuf[0][:, 1024 * dd + 512 * half:
                                   1024 * dd + 512 * half + 512],
                           dd == 0, False,
                           [f"vw{dd}", f"xbf{dd}"],
                           f"v{ly}_{tt}_{dd}_{half}")
                MM(PA, PS[PA][:, :], onesrow[0:1, 0:128],
                   brow[0:1, 0:512], False, True,
                   ["onesrow", "brow0"], f"vb{ly}_{tt}_a")
                MM(PC, PS[PC][:, :], onesrow[0:1, 0:128],
                   brow[0:1, 512:1024], False, True,
                   ["onesrow", "brow0"], f"vb{ly}_{tt}_b")
                for half, bank in ((0, PA), (1, PC)):
                    VOP("tensor_copy",
                        (vstage[:, :].rearrange("p (h c) -> p h c", c=65)
                         [:, :, 0:64],
                         PS[bank][:, :].rearrange("p (h c) -> p h c", c=64)),
                        [psr[bank]], ["vstage"], f"vev{ly}_{tt}_{half}")
                    s_, j_ = tt // 4, tt % 4
                    for a_ in range(2):
                        dstap = bass.AP(
                            agin.tensor,
                            4 * half * g.PBLK + a_ * g.VSZ
                            + (512 * s_ + j_) * 65,
                            [[4 * 65, 128], [g.PBLK, 4], [1, 65]])
                        srcap = vstage[:, :].rearrange(
                            "p (b a c) -> p b a c", a=2, c=65)[:, :, a_, :]
                        DMA("sync", dstap, srcap,
                            ["vstage"],
                            [f"aginb{4 * half + bb}" for bb in range(4)],
                            f"vst{ly}_{tt}_{half}_{a_}")

        def emit_norms_ag(ly):
            DMA("gpsimd", bass.AP(agin_st.tensor, 0, [[1, 16]]),
                maxsq[1][:, :], ["maxsq1"], ["agin_s"], f"nst{ly}")
            COLL("AllGather", ALU.bypass,
                 [agin_st.rearrange("(a b) -> a b", b=64).opt()],
                 [agout_st.rearrange("(a b) -> a b", b=64).opt()],
                 ["agin_s"], ["agout_s"], f"agst{ly}")
            for pi, (ha, hb) in enumerate(g.pairs):
                blk = ha // 2
                ain = bass.AP(agin.tensor, blk * g.PBLK,
                              [[512, g.PBLK // 512], [1, 512]])
                aout = bass.AP(agout.tensor, blk * 4 * g.PBLK,
                               [[512, 4 * g.PBLK // 512], [1, 512]])
                COLL("AllGather", ALU.bypass, [ain.opt()], [aout.opt()],
                     [f"aginb{blk}"], [f"agoutb{blk}"], f"ag{ly}_{blk}")
            DMA("gpsimd", rowscr[0:16], maxsq[0][:, :], ["maxsq0"],
                ["rowscr"], f"nbq{ly}")
            DMA("sync", rows[0:1, 16:32], rowscr[0:16], ["rowscr"],
                ["rows"], f"nbq2{ly}")
            statrow = bass.AP(agout_st.tensor, 0,
                              [[1, 1], [1, 16], [64, 4]])

            def f_st(eng):
                return eng.dma_start(out=rows[0:1, 64:128].rearrange(
                    "a (h r) -> a h r", r=4), in_=statrow,
                    allow_slow_non_contiguous=True)
            sch.add("gpsimd", f_st, reads=["agout_s"], writes=["rows"],
                    dma=True, tag=f"nks{ly}")
            VOP("reduce_max", (rows[0:1, 0:16],
                               rows[0:1, 64:128].rearrange(
                                   "a (h r) -> a h r", r=4), AX.X),
                ["rows"], ["rows"], f"nkr{ly}")
            ACT(rows[0:1, 32:64], rows[0:1, 0:32], AF.Sqrt, ["rows"],
                ["rows"], tag=f"nsqrt{ly}")
            VOP("tensor_tensor", (rows[0:1, 48:64], rows[0:1, 32:48],
                                  rows[0:1, 48:64], ALU.mult),
                ["rows"], ["rows"], f"nmul{ly}")
            VOP("tensor_scalar", (rows[0:1, 48:64], rows[0:1, 48:64],
                                  0.125, M_MARGIN, ALU.mult, ALU.add),
                ["rows"], ["rows"], f"nmbar{ly}")
            hseg = {}
            for (hh, s, u, i), c in g.bias_col.items():
                a, b = hseg.get(hh, (c, c))
                hseg[hh] = (min(a, c), max(b, c))
            nchunk = (g.NBC + 511) // 512
            for cch in range(nchunk):
                c0 = 512 * cch
                cw = min(512, g.NBC - c0)
                DMA("gpsimd", flagst[0:1, 0:cw], flag_h[0:1, c0:c0 + cw],
                    ["flag_h"], ["flagst"], f"fstg{ly}_{cch}")
                for h in range(H):
                    a, b = hseg[h]
                    b += 1
                    lo, hi = max(a, c0), min(b, c0 + cw)
                    if lo >= hi:
                        continue
                    VOP("tensor_scalar",
                        (mrowst[0:1, lo - c0:hi - c0],
                         flagst[0:1, lo - c0:hi - c0],
                         KILL, rows[0:1, 48 + h:49 + h],
                         ALU.mult, ALU.add),
                        ["flagst", "rows"], ["mrowst"], f"mrow{ly}_{cch}_{h}")
                DMA("sync", stat_f[2][:, 0:cw], rampall_h[:, c0:c0 + cw],
                    ["rampall_h"], ["stf2"], f"rstg{ly}_{cch}")
                MM(PM, PS[PM][:, 0:cw], onesrow[0:1, 0:128],
                   mrowst[0:1, 0:cw], True, True,
                   ["onesrow", "mrowst"], f"bb{ly}_{cch}")
                VOP("tensor_tensor", (biasall[:, c0:c0 + cw],
                                      stat_f[2][:, 0:cw],
                                      PS[PM][:, 0:cw], ALU.subtract),
                    ["stf2", psr[PM]], ["biasall"], f"bs{ly}_{cch}")

        def emit_attention(ly):
            VW = NBK * 65
            for pi, (ha, hb) in enumerate(g.pairs):
                bi = pi % 2
                if pi < 2:
                    old = [f"qkw{fc}" for fc in range(16)] \
                        + [f"vw{dd}" for dd in range(DCH)]
                    sch.barrier(old, kb_regions(bi) + vb_regions(bi)
                                + [f"qp{bi}"])
                DMA("sync", qpair[bi][0:64, 0:TLOC],
                    qscr[64 * ha:64 * ha + 64, :], ["qscr"], [f"qp{bi}"],
                    f"qld{ly}_{pi}a")
                DMA("sync", qpair[bi][64:128, 0:TLOC],
                    qscr[64 * hb:64 * hb + 64, :], ["qscr"], [f"qp{bi}"],
                    f"qld{ly}_{pi}b")
                for a, h in ((0, ha), (1, hb)):
                    for (dt4, pidx) in g.kload[h]:
                        m = dt4 // 4
                        GATHER(kbuf[bi][64 * a:64 * a + 64,
                                        128 * dt4:128 * dt4 + 512],
                               agout_k, idxtab[0:64, pidx:pidx + 1],
                               [f"agoutb{h // 2}", "idxtab"],
                               [f"kb{bi}g{m}"],
                               f"kg{ly}_{pi}_{a}_{m}")
                    for (dt4, pidx) in g.vload[h]:
                        m = dt4 // 4
                        GATHER(vbuf[bi][:, VW * a + 65 * dt4:
                                        VW * a + 65 * dt4 + 260],
                               agout_v, idxtab[:, pidx:pidx + 1],
                               [f"agoutb{h // 2}", "idxtab"],
                               [f"vb{bi}a{a}g{m}"],
                               f"vg{ly}_{pi}_{a}_{m}")
                erot = [0]
                for sch_s in range(NS):
                    for a, h in ((0, ha), (1, hb)):
                        W = g.Wq[h]
                        beta = W // 128
                        U = g.CH // W
                        for u in range(U):
                            n = g.nmat[(h, sch_s, u)]
                            qc = 512 * sch_s + W * u
                            rhs = qpair[bi][64 * a:64 * a + 64, qc:qc + W]
                            obanks = [PA, PB, PC, PD][:beta]
                            LAG = 3
                            epipe = []
                            for i in range(n + LAG):
                                if i < n:
                                    B = g.bslot(h, sch_s, u, i)
                                    col = g.bias_col[(h, sch_s, u, i)]
                                    sbank = [S0, S1, PM][i % 3]
                                    mk = f"kb{bi}g{B // 4}"
                                    MM(sbank, PS[sbank][:, 0:W],
                                       kbuf[bi][64 * a:64 * a + 64,
                                                128 * B:128 * B + 128],
                                       rhs, True, True,
                                       [mk, f"qp{bi}"],
                                       f"S{ly}_{pi}_{a}_{sch_s}_{u}_{i}")
                                    eb = ebuf[erot[0] % 6]
                                    ebr = f"eb{erot[0] % 6}"
                                    erot[0] += 1
                                    ACT(eb[:, 0:W], PS[sbank][:, 0:W],
                                        AF.Exp,
                                        [psr[sbank], "biasall"], [ebr],
                                        bias=biasall[:, col:col + 1],
                                        scale=0.125,
                                        tag=f"E{ly}_{pi}_{a}_{sch_s}_{u}_{i}")
                                    if i < beta:
                                        VOP("tensor_tensor",
                                            (eb[:, 0:W], eb[:, 0:W],
                                             masks[(W, i)], ALU.mult),
                                            [ebr, mask_reg[(W, i)]], [ebr],
                                            f"M{ly}_{pi}_{a}_{sch_s}_{u}_{i}")
                                    epipe.append((i, eb, ebr))
                                if i >= LAG:
                                    j, ejb, ejr = epipe[i - LAG]
                                    Bj = g.bslot(h, sch_s, u, j)
                                    mv = f"vb{bi}a{a}g{Bj // 4}"
                                    for qs in range(beta):
                                        MM(obanks[qs],
                                           PS[obanks[qs]][:, 0:65],
                                           ejb[:, 128 * qs:128 * qs + 128],
                                           vbuf[bi][:, VW * a + 65 * Bj:
                                                    VW * a + 65 * Bj + 65],
                                           j == 0, j == n - 1,
                                           [ejr, mv],
                                           f"PV{ly}_{pi}_{a}_{sch_s}_{u}_{j}_{qs}")
                            for qs in range(beta):
                                k_in_s = (W * u + 128 * qs) // 128
                                rc = rcpb[qs]
                                VOP("reciprocal",
                                    (rc[:, :], PS[obanks[qs]][:, 64:65]),
                                    [psr[obanks[qs]]], [f"rc{qs}"],
                                    f"R{ly}_{pi}_{a}_{sch_s}_{u}_{qs}")
                                VOP("tensor_scalar",
                                    (otm[sch_s % 2][:, 128 * k_in_s + 64 * a:
                                                    128 * k_in_s + 64 * a + 64],
                                     PS[obanks[qs]][:, 0:64], rc[:, 0:1],
                                     None, ALU.mult),
                                    [psr[obanks[qs]], f"rc{qs}"],
                                    [f"otm{sch_s % 2}"],
                                    f"ON{ly}_{pi}_{a}_{sch_s}_{u}_{qs}")
                    for a, h in ((0, ha), (1, hb)):
                        dstap = bass.AP(
                            oscr.tensor, (4 * sch_s) * 128 * D + 64 * h,
                            [[D, 128], [128 * D, 4], [1, 64]])
                        DMA("sync", dstap,
                            otm[sch_s % 2][:, :].rearrange(
                                "p (k c) -> p k c", c=128)
                            [:, :, 64 * a:64 * a + 64],
                            [f"otm{sch_s % 2}"], [f"oscr{sch_s}_{h}"],
                            f"ost{ly}_{pi}_{sch_s}_{a}")

        def emit_outproj(ly):
            sch.barrier(kb_regions(0) + kb_regions(1),
                        [f"ofm{d}" for d in range(DCH)])
            for q in range(NQS):
                rb = qpair[q % 2]
                if q < 2:
                    sch.barrier([f"qp{q % 2}"], [f"oin{q % 2}"])
                DMA("sync", rb[:, 0:D], oscr[q, :, :],
                    [f"oscr{q // 4}_{hh}" for hh in range(H)],
                    [f"oin{q % 2}"], f"old{ly}_{q}")
                for dd in range(DCH):
                    TRANS(TRB[:, 128 * dd:128 * dd + 128],
                          rb[:, 128 * dd:128 * dd + 128], identb[:, :],
                          [f"oin{q % 2}", "identb"], ["trb"],
                          f"otr{ly}_{q}_{dd}")
                    ofm = kbuf[dd // 4][:, TLOC * (dd % 4) + 128 * q:
                                       TLOC * (dd % 4) + 128 * q + 128]
                    VOP("tensor_copy",
                        (ofm, TRB[:, 128 * dd:128 * dd + 128]),
                        ["trb"], [f"ofm{dd}"], f"oev{ly}_{q}_{dd}")
            ob0 = ly * D
            for f in range(DCH):
                wt = wtile[f % 4]
                DMA("gpsimd",
                    wt[:, :].rearrange("p (c f) -> p c f", f=128),
                    wo_h[ly, :, 128 * f:128 * f + 128]
                    .rearrange("(c p) f -> p c f", p=128),
                    ["wo_h"], [f"wt{f % 4}"], f"wold{ly}_{f}")
                for t in range(NT):
                    bank = PA + (t % 2)
                    tc = slice(TC * t, TC * (t + 1))
                    for dd in range(DCH):
                        ofm = kbuf[dd // 4][:, TLOC * (dd % 4) + TC * t:
                                            TLOC * (dd % 4) + TC * t + TC]
                        MM(bank, PS[bank][:, :],
                           wt[:, 128 * dd:128 * dd + 128], ofm,
                           dd == 0, False,
                           [f"wt{f % 4}", f"ofm{dd}"], f"op{ly}_{f}_{t}_{dd}")
                    MM(bank, PS[bank][:, :], brow[32:33, 128 * f:
                                                  128 * f + 128],
                       ones65[32:33, :], False, True,
                       ["brow1", "ones65"], f"opb{ly}_{f}_{t}")
                    VOP("tensor_tensor",
                        (x_in[f][:, tc], x_in[f][:, tc], PS[bank][:, :],
                         ALU.add), [f"x_in{f}", psr[bank]], [f"x_in{f}"],
                        f"ores{ly}_{f}_{t}")

        def emit_ffn_half(ly, th):
            tof = HSF * th
            fb0 = ly * PCL + 48
            if th == 0:
                sch.barrier([f"ofm{d}" for d in range(DCH)]
                            + kb_regions(0) + kb_regions(1)
                            + vb_regions(0) + vb_regions(1),
                            [f"hkb{i}" for i in range(NFC1)]
                            + ["w2t0", "w2t1"])
            for fc in range(NFC1):
                wt = wtile[fc % 4]
                DMA("gpsimd",
                    wt[:, :].rearrange("p (c f) -> p c f", f=128),
                    w1_h[ly, :, 128 * fc:128 * fc + 128]
                    .rearrange("(c p) f -> p c f", p=128),
                    ["w1_h"], [f"wt{fc % 4}"], f"w1ld{ly}_{th}_{fc}")
                hb, hcol = h_slot(fc)
                for sub in range(NSUB):
                    bank = PA + ((fc if NSUB == 1 else sub) % 2)
                    tc = slice(TC * sub, TC * sub + TC)
                    for dd in range(DCH):
                        MM(bank, PS[bank][:, :],
                           wt[:, 128 * dd:128 * dd + 128],
                           xbf[dd][:, tc], dd == 0, dd == DCH - 1,
                           [f"wt{fc % 4}", f"xbf{dd}"],
                           f"f1_{ly}_{th}_{fc}_{sub}_{dd}")
                    ACT(hb[:, hcol + TC * sub:hcol + TC * sub + TC],
                        PS[bank][:, :], AF.Gelu,
                        [psr[bank], "params"], [f"hkb{fc}"],
                        bias=params[:, fb0 + fc:fb0 + fc + 1], scale=1.0,
                        tag=f"f1e{ly}_{th}_{fc}_{sub}")
            sch.barrier(["stf0", "stf1", "stf2", "biasall"],
                        ["w2t0", "w2t1"])
            for f in range(DCH):
                for hh in range(2):
                    w2t = w2half[hh]
                    DMA("gpsimd",
                        w2t.rearrange("p (c f) -> p c f", f=128),
                        w2_h[ly, 2048 * hh:2048 * (hh + 1),
                             128 * f:128 * f + 128]
                        .rearrange("(c p) f -> p c f", p=128),
                        ["w2_h"], [f"w2t{hh}"], f"w2ld{ly}_{th}_{f}_{hh}")
                for sub in range(NSUB):
                    bank = PC + ((f if NSUB == 1 else sub) % 2)
                    tc = slice(TC * sub, TC * sub + TC)
                    for hk in range(NFC1):
                        hb, hcol = h_slot(hk)
                        w2t = w2half[hk // 16]
                        MM(bank, PS[bank][:, :],
                           w2t[:, 128 * (hk % 16):128 * (hk % 16) + 128],
                           hb[:, hcol + TC * sub:hcol + TC * sub + TC],
                           hk == 0, False,
                           [f"w2t{hk // 16}", f"hkb{hk}"],
                           f"f2_{ly}_{th}_{f}_{sub}_{hk}")
                    MM(bank, PS[bank][:, :],
                       brow[64:65, 128 * f:128 * f + 128],
                       ones65[64:65, :], False, True,
                       ["brow2", "ones65"], f"f2b{ly}_{th}_{f}_{sub}")
                    VOP("tensor_tensor",
                        (x_in[f][:, tof + TC * sub:tof + TC * sub + TC],
                         x_in[f][:, tof + TC * sub:tof + TC * sub + TC],
                         PS[bank][:, :], ALU.add),
                        [f"x_in{f}", psr[bank]], [f"x_in{f}"],
                        f"f2r{ly}_{th}_{f}_{sub}")

        def emit_brow(ly):
            for rbase, rowi in ((0, 0), (32, 1), (64, 2)):
                DMA("gpsimd", brow[rbase:rbase + 1, :],
                    brow_h[rowi:rowi + 1, ly * D:(ly + 1) * D],
                    ["brow_h"], [f"brow{rowi}"], f"brow{ly}_{rowi}")

        def emit_final():
            fincol = L * PCL
            NFCG = FG // 128
            per_kb = KBC // HSF

            def hg_loc(fc, sub):
                if fc < per_kb:
                    hb, hcol = kbuf[0], fc * HSF
                elif fc < 2 * per_kb:
                    hb, hcol = kbuf[1], (fc - per_kb) * HSF
                else:
                    hb, hcol = vbuf[0], (fc - 2 * per_kb) * HSF
                return hb[:, hcol + TC * sub:hcol + TC * sub + TC], \
                    f"hkb{fc}"

            sch.barrier(sum((vb_regions(i) for i in range(2)), [])
                        + kb_regions(0) + kb_regions(1)
                        + [f"hkb{i}" for i in range(NFC1)]
                        + [f"qp{i}" for i in range(2)] + ["biasall"]
                        + ["w2t0", "w2t1"]
                        + [f"ofm{d}" for d in range(DCH)],
                        [f"xng{d}" for d in range(DCH)]
                        + [f"hkb{i}" for i in range(NFCG)]
                        + ["xtmf", "wu0", "wu1"])
            for th in range(NTH):
                tof = HSF * th
                emit_ln("fin%d" % th, th, wcol=fincol, bcol=fincol + 8)
                xng = []
                for dd in range(DCH):
                    ap = vbuf[1][:, HSF * dd:HSF * dd + HSF]
                    xng.append(ap)
                    DMA("sync", ap,
                        xnscr[128 * dd:128 * dd + 128, tof:tof + HSF],
                        ["xnscr"], [f"xng{dd}"], f"gxn{th}_{dd}")
                sch.barrier(["w2t0", "w2t1", "stf0", "stf1", "stf2"], ["wu0", "wu1"])
                for fc in range(NFCG):
                    wtg = wtile[fc % 4]
                    wtu = wtile2v[fc % 2]
                    DMA("gpsimd",
                        wtg[:, :].rearrange("p (c f) -> p c f", f=128),
                        wg_h[:, 128 * fc:128 * fc + 128]
                        .rearrange("(c p) f -> p c f", p=128),
                        ["wg_h"], [f"wt{fc % 4}"], f"wgld{th}_{fc}")
                    DMA("gpsimd",
                        wtu.rearrange("p (c f) -> p c f", f=128),
                        wu_h[:, 128 * fc:128 * fc + 128]
                        .rearrange("(c p) f -> p c f", p=128),
                        ["wu_h"], [f"wu{fc % 2}"], f"wuld{th}_{fc}")
                    for sub in range(NSUB):
                        rot = (fc if NSUB == 1 else sub) % 2
                        bg = PA + rot
                        bu = PC + rot
                        for dd in range(DCH):
                            MM(bg, PS[bg][:, :],
                               wtg[:, 128 * dd:128 * dd + 128],
                               xng[dd][:, TC * sub:TC * sub + TC],
                               dd == 0, dd == DCH - 1,
                               [f"wt{fc % 4}", f"xng{dd}"],
                               f"gg_{th}_{fc}_{sub}_{dd}")
                            MM(bu, PS[bu][:, :],
                               wtu[:, 128 * dd:128 * dd + 128],
                               xng[dd][:, TC * sub:TC * sub + TC],
                               dd == 0, dd == DCH - 1,
                               [f"wu{fc % 2}", f"xng{dd}"],
                               f"gu_{th}_{fc}_{sub}_{dd}")
                        sil = sqt[rot]
                        ACT(sil[:, :], PS[bg][:, :], AF.Silu,
                            [psr[bg]], [f"sqt{rot}"], scale=DEQ,
                            tag=f"gsil{th}_{fc}_{sub}")
                        upv = evac[rot]
                        VOP("tensor_copy", (upv, PS[bu][:, :]),
                            [psr[bu]], [f"evac{rot}"],
                            f"gup{th}_{fc}_{sub}")
                        hap, hreg = hg_loc(fc, sub)
                        VOP("tensor_tensor", (hap, sil[:, :], upv,
                                              ALU.mult),
                            [f"sqt{rot}", f"evac{rot}"], [hreg],
                            f"gh{th}_{fc}_{sub}")
                sch.barrier(["wu0", "wu1"], ["w2t0"])
                for f in range(DCH):
                    DMA("gpsimd",
                        wdtile[:, 0:128 * NFCG]
                        .rearrange("p (c f) -> p c f", f=128),
                        wd_h[:, 128 * f:128 * f + 128]
                        .rearrange("(c p) f -> p c f", p=128),
                        ["wd_h"], ["w2t0"], f"wdld{th}_{f}")
                    for sub in range(NSUB):
                        rot = (f if NSUB == 1 else sub) % 2
                        bank = PA + rot
                        for hk in range(NFCG):
                            hap, hreg = hg_loc(hk, sub)
                            MM(bank, PS[bank][:, :],
                               wdtile[:, 128 * hk:128 * hk + 128],
                               hap, hk == 0, hk == NFCG - 1,
                               ["w2t0", hreg],
                               f"gd{th}_{f}_{sub}_{hk}")
                        gt = evac[2 + rot]
                        ACT(gt, PS[bank][:, :], AF.Silu,
                            [psr[bank]], [f"evac{2 + rot}"],
                            scale=DEQ * DEQ, tag=f"gout{th}_{f}_{sub}")
                        tc = slice(tof + TC * sub, tof + TC * sub + TC)
                        ltc = slice(TC * sub, TC * sub + TC)
                        VOP("tensor_tensor",
                            (x_in[f][:, tc], xbf[f][:, ltc], gt,
                             ALU.mult),
                            [f"xbf{f}", f"evac{2 + rot}"], [f"x_in{f}"],
                            f"gc{th}_{f}_{sub}")
                for q in range(tof // 128, (tof + HSF) // 128):
                    DMA("sync", xtmf, x_h[128 * q:128 * q + 128, :],
                        ["x_h"], ["xtmf"], f"fxld{q}")
                    for dd in range(DCH):
                        bank = PA if dd < 4 else PB
                        TRANS(PS[bank][:, 128 * (dd % 4):
                                       128 * (dd % 4) + 128],
                              x_in[dd][:, 128 * q:128 * q + 128],
                              identf[:, :],
                              [f"x_in{dd}", "identf"], [psr[bank]],
                              f"ftr{q}_{dd}")
                    ot = outtm[q % 2]
                    VOP("tensor_tensor", (ot[:, 0:512], PS[PA][:, :],
                                          xtmf[:, 0:512], ALU.add),
                        [psr[PA], "xtmf"], [f"outtm{q % 2}"], f"fc0_{q}")
                    VOP("tensor_tensor", (ot[:, 512:1024], PS[PB][:, :],
                                          xtmf[:, 512:1024], ALU.add),
                        [psr[PB], "xtmf"], [f"outtm{q % 2}"], f"fc1_{q}")
                    DMA("sync", out_h[128 * q:128 * q + 128, :], ot[:, :],
                        [f"outtm{q % 2}"], ["out_h"], f"fst{q}")

        for ly in range(L):
            lc = ly * PCL
            emit_brow(ly)
            emit_qkv_weights(ly)
            for th in range(NTH):
                emit_ln(f"l{ly}a{th}", th, wcol=lc, bcol=lc + 8)
                emit_qkv_half(ly, th)
            emit_norms_ag(ly)
            emit_attention(ly)
            emit_outproj(ly)
            for th in range(NTH):
                emit_ln(f"l{ly}b{th}", th, wcol=lc + 16, bcol=lc + 24)
                emit_ffn_half(ly, th)
        sch.barrier([f"qp{i}" for i in range(2)] + ["oin0", "oin1"],
                    ["outtm0", "outtm1"])
        emit_final()
        sch.finalize(final_engine="sync")
    return nc


def _pack_params(g, ins):
    L, F = g.L, g.F
    PCL = 32 + 16 + F // 128
    cols = []

    def chunked(v):
        return np.ascontiguousarray(v.reshape(-1, 128).T)

    out = np.zeros((128, L * PCL + 16), dtype=np.float32)
    for ly in range(L):
        c = ly * PCL
        out[:, c:c + 8] = chunked(ins["L0_attn_ln1_w"][ly])
        out[:, c + 8:c + 16] = chunked(ins["L0_attn_ln1_b"][ly])
        out[:, c + 16:c + 24] = chunked(ins["L0_attn_ln2_w"][ly])
        out[:, c + 24:c + 32] = chunked(ins["L0_attn_ln2_b"][ly])
        out[:, c + 32:c + 48] = chunked(ins["L0_attn_in_proj_b"][ly][:2048])
        out[:, c + 48:c + 48 + F // 128] = chunked(ins["L0_attn_ffn1_b"][ly])
    out[:, L * PCL:L * PCL + 8] = chunked(ins["L0_attn_final_w"])
    out[:, L * PCL + 8:L * PCL + 16] = chunked(ins["L0_attn_final_b"])
    return out


def _host_tables(g, gi):
    flag = np.zeros((1, g.NBC), dtype=np.float32)
    for (h, s, u, i), c in g.bias_col.items():
        beta = g.Wq[h] // 128
        t = (4 * s + gi) * 4 + (u + 1) * beta - 1 - i
        if t < 0:
            flag[0, c] = 1.0
    idx = np.zeros((128, g.NIDX), dtype=np.uint32)
    for p, (kind, h, m) in enumerate(g.idx_plan):
        cg = m + (g.bmin[h] - g.padlo[h]) // 4 + gi
        cg = min(max(cg, 0), g.NCH - 1)
        r, sp = cg % 4, cg // 4
        blk, a = h // 2, h % 2
        rbase = blk * 4 * g.PBLK + r * g.PBLK
        if kind == "K":
            base = (rbase + 2 * g.VSZ + a * g.KSZ) // 512
            rows = base + np.arange(64) * (g.TLOC // 512) + sp
            idx[0:64, p] = rows
        else:
            base = (rbase + a * g.VSZ) // 260
            idx[:, p] = base + 128 * sp + np.arange(128)
    return flag, idx


_GEOM = None
_NC = None


def prepare_in_maps(g, ins):
    x = np.asarray(ins["L0_attn_x"], dtype=np.float32)
    wqkv = np.ascontiguousarray(
        np.asarray(ins["L0_attn_in_proj_w"], np.float32).transpose(0, 2, 1))
    wo = np.ascontiguousarray(
        np.asarray(ins["L0_attn_out_proj_w"], np.float32).transpose(0, 2, 1))
    w1 = np.ascontiguousarray(
        np.asarray(ins["L0_attn_ffn1_w"], np.float32).transpose(0, 2, 1))
    w2 = np.ascontiguousarray(
        np.asarray(ins["L0_attn_ffn2_w"], np.float32).transpose(0, 2, 1))
    wg = np.ascontiguousarray(np.asarray(ins["L0_ffn_w_gate"], np.int32).T)
    wu = np.ascontiguousarray(np.asarray(ins["L0_ffn_w_up"], np.int32).T)
    wd = np.ascontiguousarray(np.asarray(ins["L0_ffn_w_down"], np.int32).T)
    params = _pack_params(g, ins)
    brow = np.stack([
        np.ascontiguousarray(
            np.asarray(ins["L0_attn_in_proj_b"], np.float32)[:, 2048:3072]
        ).reshape(-1),
        np.asarray(ins["L0_attn_out_proj_b"], np.float32).reshape(-1),
        np.asarray(ins["L0_attn_ffn2_b"], np.float32).reshape(-1),
    ]).astype(np.float32)
    cpack, rampall = host_consts(g)
    in_maps = []
    for c in range(8):
        seq, gi = c // 4, c % 4
        slab = np.ascontiguousarray(np.concatenate(
            [x[seq, (4 * s + gi) * g.CH:(4 * s + gi + 1) * g.CH]
             for s in range(g.NS)], axis=0))
        flag, idx = _host_tables(g, gi)
        in_maps.append({
            "x_slab": slab, "wqkv": wqkv, "wo": wo, "w1": w1, "w2": w2,
            "wg": wg, "wu": wu, "wd": wd, "params": params, "brow": brow,
            "flagrow": flag, "idxtab": idx, "cpack": cpack,
            "rampall": rampall,
        })
    return in_maps


def assemble_out(g, slabs):
    out = np.zeros((2, g.T, g.D), dtype=np.float32)
    for c in range(8):
        seq, gi = c // 4, c % 4
        for s in range(g.NS):
            out[seq, (4 * s + gi) * g.CH:(4 * s + gi + 1) * g.CH] = \
                slabs[c][s * g.CH:(s + 1) * g.CH]
    return out


def kernel(**ins):
    global _GEOM, _NC
    g = _GEOM or Geom(T=8192, depth=4)
    _GEOM = g
    if _NC is None:
        _NC = build_program(g)
    in_maps = prepare_in_maps(g, ins)
    res = run_bass_kernel_spmd(_NC, in_maps, core_ids=list(range(8)))
    return assemble_out(g, [res.results[c]["out_slab"] for c in range(8)])
